# version 03
# baseline (speedup 1.0000x reference)
import numpy as np
import ml_dtypes
import concourse.bass as bass
import concourse.mybir as mybir
from concourse.bass_utils import run_bass_kernel_spmd
from concourse.expressions import make_scalar_value

F32 = mybir.dt.float32
BF16 = mybir.dt.bfloat16
U32 = mybir.dt.uint32
AF = mybir.ActivationFunctionType
ALU = mybir.AluOpType
bf16 = ml_dtypes.bfloat16

B = 4
DM = 1024
DI = 2048
DH = 1024
S = 16
RK = 64
NDB = DH // 128
TC = 1024


def build_program(T, debug=False):
    NCH = T // TC
    NSUB = TC // 128
    nc = bass.Bass("TRN2", target_bir_lowering=False, detect_race_conditions=False)

    def din(name, shape, dt=F32):
        return nc.dram_tensor(name, shape, dt, kind="ExternalInput").ap()

    xb = din("xb", [T, DM])
    xbt = din("xbt", [DM, T])
    win_t = din("win_t", [DM, 2 * DH])
    wx_t = din("wx_t", [DH, 96])
    wdt_t = din("wdt_t", [RK, DH])
    bdt_in = din("bdt_in", [128, NDB])
    alog_in = din("alog_in", [128, NDB * S])
    convw_in = din("convw_in", [128, NDB * 4])
    convb_in = din("convb_in", [128, NDB])
    dvec_in = din("dvec_in", [128, NDB])
    wout_t = din("wout_t", [DI, DM])
    wr_t = din("wr_t", [DM, DM])
    wi_t = din("wi_t", [DM, DM])
    ct_in = din("ct_in", [DM, DM], BF16)
    st_in = din("st_in", [DM, DM], BF16)
    fbias_in = din("fbias_in", [128, NDB])
    ident_in = din("ident_in", [128, 128], BF16)
    selrep_in = din("selrep_in", [32, 32 * 128], BF16)
    role_in = din("role_in", [1, 4], U32)
    flagv_in = din("flagv_in", [1, 2 * NCH], U32)

    out_d = nc.dram_tensor("out_d", [NCH // 2, DM, TC], F32, kind="ExternalOutput").ap()

    dbg = {}
    if debug:
        for nm, shape, dt_ in [("dbg_xln", [128, 8 * TC], BF16),
                               ("dbg_uc", [128, NDB * TC], BF16),
                               ("dbg_zs", [128, NDB * TC], BF16),
                               ("dbg_u", [128, NDB * (TC + 4)], BF16),
                               ("dbg_xdbl", [128, TC], F32),
                               ("dbg_dt", [128, TC], BF16),
                               ("dbg_dtu", [128, TC], BF16),
                               ("dbg_brep", [128, TC], BF16),
                               ("dbg_crep", [128, TC], BF16),
                               ("dbg_yfin", [128, TC], BF16),
                               ("dbg_m", [128, NDB * DM], BF16),
                               ("dbg_hst", [128, 128], F32),
                               ("dbg_cv", [128, TC], BF16),
                               ("dbg_xtm", [128, TC], BF16),
                               ("dbg_stat", [128, 8], F32)]:
            dbg[nm] = nc.dram_tensor(nm, shape, dt_, kind="ExternalOutput").ap()
    win_b = nc.dram_tensor("win_b", [DM, 2 * DH], BF16, kind="Internal").ap()
    wout_b = nc.dram_tensor("wout_b", [DI, DM], BF16, kind="Internal").ap()

    sh_xdbl = nc.dram_tensor("sh_xdbl", [2, NCH, 96, TC], F32, kind="Internal",
                             addr_space="Shared").ap()
    sh_y = nc.dram_tensor("sh_y", [2, NCH, DH, TC], BF16, kind="Internal",
                          addr_space="Shared").ap()
    sh_flag = nc.dram_tensor("sh_flag", [4, NCH], U32, kind="Internal",
                             addr_space="Shared").ap()

    ctx = []

    def sbuf(name, shape, dtype):
        g = nc.sbuf_tensor(name, shape, dtype)
        t = g.__enter__()
        ctx.append(g)
        return t[:, :] if hasattr(t, "__getitem__") else t

    def psum(name, shape):
        g = nc.psum_tensor(name, shape, F32)
        t = g.__enter__()
        ctx.append(g)
        return t[:, :]

    def mksem(name, num=None):
        g = nc.semaphore(name, num=num)
        t = g.__enter__()
        ctx.append(g)
        return t

    bigscr = sbuf("bigscr", [128, 32 * TC], BF16)
    u_t = sbuf("u_t", [128, NDB * (TC + 4)], BF16)
    zs_t = sbuf("zs_t", [128, NDB * TC], BF16)
    uc_t = sbuf("uc_t", [128, NDB * TC], BF16)
    m_sb = sbuf("m_sb", [128, NDB * DM], BF16)
    dA_t = sbuf("dA_t", [128, 2 * TC], BF16)
    b_t = sbuf("b_t", [128, TC], BF16)
    h_t = sbuf("h_t", [128, TC], BF16)
    mm_t = sbuf("mm_t", [128, 2 * TC], BF16)
    yfin_t = sbuf("yfin_t", [128, 2 * TC], BF16)
    dtb_t = sbuf("dtb_t", [128, TC], BF16)
    dtu_t = sbuf("dtu_t", [128, TC], BF16)
    sg_scr = sbuf("sg_scr", [128, TC], F32)
    sp_scr = sbuf("sp_scr", [128, TC], F32)
    cfin_b = sbuf("cfin_b", [128, 512], BF16)
    x_sub = sbuf("x_sub", [128, DM], F32)
    xdbl_t = sbuf("xdbl_t", [128, TC], F32)
    dtr_b = sbuf("dtr_b", [64, TC], BF16)
    bc_b = sbuf("bc_b", [32, TC], BF16)
    win_blk = sbuf("win_blk", [128, 2 * NDB * 128], BF16)
    wout_blk = sbuf("wout_blk", [128, 2 * DM], BF16)
    ya_t = sbuf("ya_t", [128, 2 * 512], BF16)
    xfm_t = sbuf("xfm_t", [128, 512], F32)
    xfm_b = sbuf("xfm_b", [128, 2 * 512], BF16)
    outet_t = sbuf("outet_t", [128, 512], F32)
    ident = sbuf("ident", [128, 128], BF16)
    ones_col = sbuf("ones_col", [128, 1], BF16)
    ones_row = sbuf("ones_row", [1, 128], BF16)
    bdt_sb = sbuf("bdt_sb", [128, NDB], F32)
    aneg_sb = sbuf("aneg_sb", [128, NDB * S], F32)
    convw_sb = sbuf("convw_sb", [128, NDB * 4], F32)
    convb_sb = sbuf("convb_sb", [128, NDB], F32)
    dvec_sb = sbuf("dvec_sb", [128, NDB], F32)
    fbias_sb = sbuf("fbias_sb", [128, NDB], F32)
    role_sb = sbuf("role_sb", [1, 4], U32)
    flagv_sb = sbuf("flagv_sb", [1, 2 * NCH], U32)
    pollbuf = sbuf("pollbuf", [1, 4], U32)
    hstate = sbuf("hstate", [128, 128], F32)
    wxb_sb = sbuf("wxb_sb", [128, NDB * 96], BF16)
    selrep = sbuf("selrep", [32, 32 * 128], BF16)
    wdtb_sb = sbuf("wdtb_sb", [64, DH], BF16)
    mbar_sb = sbuf("mbar_sb", [128, NDB], F32)
    stat_s = sbuf("stat_s", [128, 8], F32)
    eps_t = sbuf("eps_t", [128, 1], F32)
    musig = sbuf("musig", [1, 3 * 512], F32)
    rowbf = sbuf("rowbf", [1, 2 * 512], BF16)

    xdblp_t = None

    def brep(s):
        return bigscr[:, s * TC:(s + 1) * TC]

    def crep(s):
        return bigscr[:, (16 + s) * TC:(17 + s) * TC]

    def xlnf(kt):
        return bigscr[:, (16 + kt) * TC:(17 + kt) * TC]

    xln_tm = bigscr[:, 0:TC]
    xdblp_t = bigscr[:, 0:2 * TC].bitcast(F32)
    sqtmp = bigscr[:, TC:2 * TC]
    castbuf = zs_t[:, 0:4 * TC].bitcast(F32)
    castout = u_t[:, 0:2 * TC]

    def wrb(kt):
        return uc_t[:, kt * DM:(kt + 1) * DM]

    def wib(kt):
        return bigscr[:, (16 + kt) * DM:(17 + kt) * DM]

    def ctb(kt):
        return zs_t[:, kt * DM:(kt + 1) * DM]

    def stbl(kt):
        return u_t[:, kt * (TC + 4):kt * (TC + 4) + DM]

    def Ureg(kt):
        return bigscr[:, kt * DM:(kt + 1) * DM]

    def Vreg(kt):
        return bigscr[:, (24 + kt) * DM:(25 + kt) * DM]

    big32 = bigscr.bitcast(F32)

    def x1f(et):
        return big32[:, et * 512:(et + 1) * 512]

    def x1b(et):
        return bigscr[:, 8 * TC + et * 512:8 * TC + (et + 1) * 512]

    def x1sq(et):
        return bigscr[:, 12 * TC + et * 512:12 * TC + (et + 1) * 512]

    s2rep = bigscr[:, 16 * TC:16 * TC + 512]
    mu2rep = bigscr[:, 16 * TC + 512:17 * TC]

    banks = [psum(f"psb{i}", [128, 512]) for i in range(8)]
    ps_tp = banks[0]
    ps_a = banks[1]
    ps_b = banks[2]
    ps_y0 = banks[3]
    ps_y1 = banks[4]
    ps_st = banks[5]
    ps_q = banks[6]
    ps_r = banks[7]

    s_ld = mksem("s_ld")
    s_st = mksem("s_st")
    s_pe = mksem("s_pe")
    s_act = mksem("s_act")
    s_dve = mksem("s_dve")
    s_pol = mksem("s_pol", num=200)
    SEMS = {"ld": s_ld, "st": s_st, "pe": s_pe, "act": s_act, "dve": s_dve}
    C = {"ld": 0, "st": 0, "pe": 0, "act": 0, "dve": 0}
    steps = {"sync": [], "tensor": [], "scalar": [], "vector": []}

    def add(eng, body, waits=(), inc=None):
        wl = tuple((SEMS[n], v) for (n, v) in waits if v > 0)
        if inc is None:
            def run(e, wl=wl, body=body):
                for sem_, v in wl:
                    e.wait_ge(sem_, v)
                body(e)
            steps[eng].append(run)
            return None
        amt = 16 if inc in ("ld", "st") else 1
        C[inc] += amt
        sem_i = SEMS[inc]

        def run(e, wl=wl, body=body, sem_i=sem_i, amt=amt):
            for sem_, v in wl:
                e.wait_ge(sem_, v)
            inst = body(e)
            inst.then_inc(sem_i, amt)
        steps[eng].append(run)
        return C[inc]

    for dst, src in [(ident, ident_in), (selrep, selrep_in), (bdt_sb, bdt_in), (aneg_sb, alog_in),
                     (convw_sb, convw_in), (convb_sb, convb_in), (dvec_sb, dvec_in),
                     (fbias_sb, fbias_in), (role_sb, role_in), (flagv_sb, flagv_in)]:
        add("sync", lambda e, d=dst, s=src: e.dma_start(out=d, in_=s), inc="ld")
    LD_PRO = C["ld"]

    add("scalar", lambda e: e.activation(out=aneg_sb, in_=aneg_sb, func=AF.Exp),
        waits=(("ld", LD_PRO),), inc="act")
    ACT_PRO = C["act"]

    def dve_pro(e):
        e.tensor_scalar_mul(out=aneg_sb, in0=aneg_sb, scalar1=-1.0)
        e.memset(ones_col, 1.0)
        e.memset(ones_row, 1.0)
        e.memset(hstate, 0.0)
        e.memset(eps_t, 1e-5)
        return e.memset(pollbuf, 0)
    add("vector", dve_pro, waits=(("act", ACT_PRO), ("ld", LD_PRO)), inc="dve")

    cast_rounds = ([("win", i) for i in range(8)] + [("wout", i) for i in range(16)]
                   + [("wx", i) for i in range(NDB)] + [("wr", i) for i in range(NDB)]
                   + [("wi", i) for i in range(NDB)] + [("wdt", 0)])
    for kind, i in cast_rounds:
        if kind == "win":
            src, cols = win_t[i * 128:(i + 1) * 128, :], 2 * DH
        elif kind == "wout":
            src, cols = wout_t[i * 128:(i + 1) * 128, :], DM
        elif kind == "wx":
            src, cols = wx_t[i * 128:(i + 1) * 128, :], 96
        elif kind == "wr":
            src, cols = wr_t[i * 128:(i + 1) * 128, :], DM
        elif kind == "wdt":
            src, cols = wdt_t, DH
        else:
            src, cols = wi_t[i * 128:(i + 1) * 128, :], DM
        dst_cb = castbuf[0:64, 0:DH] if kind == "wdt" else castbuf[:, 0:cols]
        ldn = add("sync", lambda e, s=src, d=dst_cb: e.dma_start(out=d, in_=s),
                  waits=(("dve", C["dve"]), ("st", C["st"])), inc="ld")
        if kind == "win":
            dv = add("vector", lambda e: e.tensor_copy(out=castout[:, 0:2 * DH],
                                                       in_=castbuf[:, 0:2 * DH]),
                     waits=(("ld", ldn),), inc="dve")
            add("sync", lambda e, i=i: e.dma_start(out=win_b[i * 128:(i + 1) * 128, :],
                                                   in_=castout[:, 0:2 * DH]),
                waits=(("dve", dv),), inc="st")
        elif kind == "wout":
            dv = add("vector", lambda e: e.tensor_copy(out=castout[:, 0:DM],
                                                       in_=castbuf[:, 0:DM]),
                     waits=(("ld", ldn),), inc="dve")
            add("sync", lambda e, i=i: e.dma_start(out=wout_b[i * 128:(i + 1) * 128, :],
                                                   in_=castout[:, 0:DM]),
                waits=(("dve", dv),), inc="st")
        elif kind == "wx":
            add("vector", lambda e, i=i: e.tensor_copy(out=wxb_sb[:, i * 96:(i + 1) * 96],
                                                       in_=castbuf[:, 0:96]),
                waits=(("ld", ldn),), inc="dve")
        elif kind == "wdt":
            add("vector", lambda e: e.tensor_copy(out=wdtb_sb,
                                                  in_=castbuf[0:64, 0:DH]),
                waits=(("ld", ldn),), inc="dve")
        elif kind == "wr":
            add("vector", lambda e, i=i: e.tensor_copy(out=wrb(i), in_=castbuf[:, 0:DM]),
                waits=(("ld", ldn),), inc="dve")
        else:
            add("vector", lambda e, i=i: e.tensor_copy(out=wib(i), in_=castbuf[:, 0:DM]),
                waits=(("ld", ldn),), inc="dve")
    DVE_CAST = C["dve"]

    for kt in range(NDB):
        add("sync", lambda e, kt=kt: e.dma_start(out=ctb(kt),
                                                 in_=ct_in[kt * 128:(kt + 1) * 128, :]),
            waits=(("dve", DVE_CAST),), inc="ld")
        add("sync", lambda e, kt=kt: e.dma_start(out=stbl(kt),
                                                 in_=st_in[kt * 128:(kt + 1) * 128, :]),
            inc="ld")
    LD_TBL = C["ld"]

    for which in range(2):
        for rt in range(NDB):
            for half in range(2):
                cs = slice(half * 512, (half + 1) * 512)
                if which == 0:
                    def mmf(e, rt=rt, cs=cs):
                        for kt in range(NDB):
                            e.matmul(out=ps_a, lhsT=wrb(kt)[:, rt * 128:(rt + 1) * 128],
                                     rhs=ctb(kt)[:, cs], start=(kt == 0),
                                     stop=(kt == NDB - 1))
                        last = None
                        for kt in range(NDB):
                            last = e.matmul(out=ps_b,
                                            lhsT=wib(kt)[:, rt * 128:(rt + 1) * 128],
                                            rhs=stbl(kt)[:, cs], start=(kt == 0),
                                            stop=(kt == NDB - 1))
                        return last.then_inc(s_pe, 1)
                else:
                    def mmf(e, rt=rt, cs=cs):
                        n = 0
                        last = None
                        for lf, rf in [(wib, ctb), (wrb, stbl)]:
                            for kt in range(NDB):
                                n += 1
                                last = e.matmul(out=ps_a,
                                                lhsT=lf(kt)[:, rt * 128:(rt + 1) * 128],
                                                rhs=rf(kt)[:, cs], start=(n == 1),
                                                stop=(n == 16))
                        return last.then_inc(s_pe, 1)
                C["pe"] += 1
                pe_n = C["pe"]
                add("tensor", mmf, waits=(("ld", LD_TBL), ("dve", C["dve"])))
                if which == 0:
                    def uevac(e, rt=rt, half=half):
                        e.tensor_copy(out=cfin_b, in_=ps_b)
                        return e.tensor_tensor(
                            out=Ureg(rt)[:, half * 512:(half + 1) * 512],
                            in0=ps_a, in1=cfin_b, op=ALU.subtract)
                    add("vector", uevac, waits=(("pe", pe_n),), inc="dve")
                else:
                    add("vector", lambda e, rt=rt, half=half: e.tensor_copy(
                        out=Vreg(rt)[:, half * 512:(half + 1) * 512], in_=ps_a),
                        waits=(("pe", pe_n),), inc="dve")
    DVE_UV = C["dve"]

    for nt in range(NDB):
        for half in range(2):
            cs = slice(half * 512, (half + 1) * 512)

            def mmf(e, nt=nt, cs=cs):
                n = 0
                last = None
                for lf, rf in [(ctb, Ureg), (stbl, Vreg)]:
                    for rt in range(NDB):
                        n += 1
                        last = e.matmul(out=ps_a, lhsT=lf(rt)[:, nt * 128:(nt + 1) * 128],
                                        rhs=rf(rt)[:, cs], start=(n == 1), stop=(n == 16))
                return last.then_inc(s_pe, 1)
            C["pe"] += 1
            pe_n = C["pe"]
            add("tensor", mmf, waits=(("dve", DVE_UV), ("act", C["act"])))
            add("scalar", lambda e, nt=nt, half=half: e.activation(
                out=m_sb[:, nt * DM + half * 512:nt * DM + (half + 1) * 512],
                in_=ps_a, func=AF.Copy, scale=1.0 / DM),
                waits=(("pe", pe_n),), inc="act")
    ACT_M = C["act"]

    for ot in range(NDB):
        def mmf(e, ot=ot):
            last = None
            for nt in range(NDB):
                last = e.matmul(out=ps_st[:, 0:1],
                                lhsT=m_sb[:, nt * DM + ot * 128:nt * DM + (ot + 1) * 128],
                                rhs=ones_col, start=(nt == 0), stop=(nt == NDB - 1))
            return last.then_inc(s_pe, 1)
        C["pe"] += 1
        pe_n = C["pe"]
        add("tensor", mmf, waits=(("act", ACT_M), ("dve", C["dve"])))
        add("vector", lambda e, ot=ot: e.tensor_copy(out=mbar_sb[:, ot:ot + 1],
                                                     in_=ps_st[:, 0:1]),
            waits=(("pe", pe_n),), inc="dve")
    PE_M = C["pe"]

    def clr(e):
        last = None
        for dbt in range(NDB):
            last = e.memset(u_t[:, dbt * (TC + 4):dbt * (TC + 4) + 4], 0.0)
        return last.then_inc(s_dve, 1)
    C["dve"] += 1
    add("vector", clr, waits=(("pe", PE_M),))
    DVE_CLR = C["dve"]

    sync_regs = {}

    def get_sync_regs(e):
        if "role" not in sync_regs:
            e.wait_ge(s_ld, LD_PRO)
            r = e.alloc_register("r_role")
            e.reg_load(r, role_sb[0:1, 0:1])
            sync_regs["role"] = make_scalar_value(r, min_val=0, max_val=1)
            sync_regs["fl"] = e.alloc_register("r_fl")
            sync_regs["cnt"] = e.alloc_register("r_cnt")
            e.reg_mov(sync_regs["cnt"], 0)
            sync_regs["tgt"] = e.alloc_register("r_tgt")
        return sync_regs

    def emit_poll(e, row_if, row_else, col, fvcol, slot):
        rg = get_sync_regs(e)
        e.reg_load(rg["tgt"], flagv_sb[0:1, fvcol:fvcol + 1])
        tgtv = make_scalar_value(rg["tgt"], min_val=0, max_val=(1 << 30))
        e.reg_mov(rg["fl"], 0)
        flv = make_scalar_value(rg["fl"], min_val=0, max_val=(1 << 30))
        cntv = make_scalar_value(rg["cnt"], min_val=0, max_val=(1 << 30))

        def one(row):
            with e.While(lambda: flv != tgtv):
                e.dma_start(out=pollbuf[0:1, slot:slot + 1],
                            in_=sh_flag[row:row + 1, col:col + 1]).then_inc(s_pol, 16)
                e.reg_add(rg["cnt"], rg["cnt"], 16)
                e.wait_ge(s_pol, cntv)
                e.reg_load(rg["fl"], pollbuf[0:1, slot:slot + 1])
        if row_if == row_else:
            one(row_if)
        else:
            with e.If(rg["role"] == 0):
                one(row_if)
            with e.Else():
                one(row_else)
            nc.end_ifs()

    for k in range(NCH):
        par = k % 2

        for sub in range(NSUB):
            ldn = add("sync", lambda e, k=k, sub=sub: e.dma_start(
                out=x_sub, in_=xb[k * TC + sub * 128:k * TC + (sub + 1) * 128, :]),
                waits=(("act", C["act"]), ("dve", DVE_CLR if k == 0 else C["dve"])),
                inc="ld")
            a1 = add("scalar", lambda e: e.activation(out=sqtmp, in_=x_sub,
                                                      func=AF.Square,
                                                      accum_out=stat_s[:, 1:2]),
                     waits=(("ld", ldn), ("dve", C["dve"])), inc="act")
            a2 = add("scalar", lambda e: e.activation(out=sqtmp, in_=x_sub,
                                                      func=AF.Copy,
                                                      accum_out=stat_s[:, 0:1]),
                     inc="act")

            def st1a(e):
                e.tensor_scalar_mul(out=stat_s[:, 2:3], in0=stat_s[:, 0:1],
                                    scalar1=1.0 / DM)
                return e.tensor_scalar_mul(out=stat_s[:, 3:4], in0=stat_s[:, 1:2],
                                           scalar1=1.0 / DM)
            w1 = add("vector", st1a, waits=(("act", a2),), inc="dve")
            w2 = add("vector", lambda e: e.tensor_tensor(
                out=stat_s[:, 4:5], in0=stat_s[:, 2:3], in1=stat_s[:, 2:3],
                op=ALU.mult), waits=(("dve", w1),), inc="dve")
            d1 = add("vector", lambda e: e.tensor_tensor(
                out=stat_s[:, 4:5], in0=stat_s[:, 3:4], in1=stat_s[:, 4:5],
                op=ALU.subtract), waits=(("dve", w2),), inc="dve")
            a3 = add("scalar", lambda e: e.activation(out=stat_s[:, 5:6],
                                                      in_=stat_s[:, 4:5],
                                                      func=AF.Sqrt, bias=eps_t),
                     waits=(("dve", d1),), inc="act")

            w3 = add("vector", lambda e: e.reciprocal(out=stat_s[:, 6:7],
                                                      in_=stat_s[:, 5:6]),
                     waits=(("act", a3),), inc="dve")
            w4 = add("vector", lambda e: e.tensor_tensor(
                out=stat_s[:, 7:8], in0=stat_s[:, 2:3], in1=stat_s[:, 6:7],
                op=ALU.mult), waits=(("dve", w3),), inc="dve")
            d2 = add("vector", lambda e: e.tensor_scalar_mul(
                out=stat_s[:, 7:8], in0=stat_s[:, 7:8], scalar1=-1.0),
                waits=(("dve", w4),), inc="dve")
            a4 = add("scalar", lambda e: e.activation(out=xln_tm, in_=x_sub,
                                                      func=AF.Identity,
                                                      scale=stat_s[:, 6:7],
                                                      bias=stat_s[:, 7:8]),
                     waits=(("dve", d2),), inc="act")
            if debug and k == 0 and sub == 0:
                add("sync", lambda e: e.dma_start(out=dbg["dbg_xtm"], in_=xln_tm),
                    waits=(("act", a4),), inc="st")
                add("sync", lambda e: e.dma_start(out=dbg["dbg_stat"], in_=stat_s),
                    waits=(("act", a4),), inc="st")
            for g in range(2):
                def tp(e, g=g):
                    tpb = ps_tp.bitcast(BF16)
                    last = None
                    for i in range(4):
                        blk = g * 4 + i
                        last = e.transpose(out=tpb[:, i * 128:(i + 1) * 128],
                                           in_=xln_tm[:, blk * 128:(blk + 1) * 128],
                                           identity=ident)
                    return last.then_inc(s_pe, 1)
                C["pe"] += 1
                pe_n = C["pe"]
                add("tensor", tp, waits=(("act", C["act"]),))
                a_ev = add("scalar", lambda e: e.activation(
                    out=cfin_b, in_=ps_tp.bitcast(BF16)[:, 0:512], func=AF.Copy),
                    waits=(("pe", pe_n), ("st", C["st"])), inc="act")
                add("sync", lambda e, g=g, sub=sub: e.dma_start(
                    out=bigscr.rearrange("p (c t) -> p c t", t=TC)[
                        :, 16 + g * 4:16 + (g + 1) * 4, sub * 128:(sub + 1) * 128],
                    in_=cfin_b.rearrange("p (c t) -> p c t", t=128)),
                    waits=(("act", a_ev),), inc="st")
        ACT_LN = C["act"]

        if k > 0:
            def halo(e):
                last = None
                for dbt in range(NDB):
                    base = dbt * (TC + 4)
                    last = e.tensor_copy(out=u_t[:, base + 1:base + 4],
                                         in_=u_t[:, base + TC + 1:base + TC + 4])
                return last.then_inc(s_dve, 1)
            C["dve"] += 1
            add("vector", halo, waits=(("act", ACT_LN),))
        DVE_HALO = C["dve"]

        for et in range(16):
            wb = et % 2
            ldn = add("sync", lambda e, et=et, wb=wb: e.dma_start(
                out=win_blk[:, wb * NDB * 128:(wb + 1) * NDB * 128].rearrange(
                    "p (kt e) -> p kt e", e=128),
                in_=win_b.rearrange("(kt p) e -> p kt e", p=128)[
                    :, :, et * 128:(et + 1) * 128]),
                waits=(("pe", max(0, C["pe"] - 1)),), inc="ld")
            for half in range(2):
                p = ps_a if half == 0 else ps_b

                def mmf(e, wb=wb, half=half, p=p):
                    last = None
                    for kt in range(NDB):
                        last = e.matmul(out=p,
                                        lhsT=win_blk[:, (wb * NDB + kt) * 128:
                                                     (wb * NDB + kt + 1) * 128],
                                        rhs=xlnf(kt)[:, half * 512:(half + 1) * 512],
                                        start=(kt == 0), stop=(kt == NDB - 1))
                    return last.then_inc(s_pe, 1)
                C["pe"] += 1
                pe_n = C["pe"]
                add("tensor", mmf, waits=(("ld", ldn), ("act", C["act"])))
                if et < 8:
                    a_uev = add("scalar", lambda e, et=et, half=half, p=p: e.activation(
                        out=u_t[:, et * (TC + 4) + 4 + half * 512:
                                et * (TC + 4) + 4 + (half + 1) * 512],
                        in_=p, func=AF.Copy),
                        waits=(("pe", pe_n), ("dve", DVE_HALO)), inc="act")
                    if half == 1:
                        dbt = et
                        base = dbt * (TC + 4)

                        def convf(e, dbt=dbt, base=base):
                            cv = sp_scr.bitcast(BF16)
                            tv = sg_scr.bitcast(BF16)
                            e.tensor_scalar(out=cv[:, 0:TC],
                                            in0=u_t[:, base + 1:base + 1 + TC],
                                            scalar1=convw_sb[:, dbt * 4:dbt * 4 + 1],
                                            scalar2=convb_sb[:, dbt:dbt + 1],
                                            op0=ALU.mult, op1=ALU.add)
                            last = None
                            for kk in range(1, 4):
                                e.tensor_scalar(
                                    out=tv[:, 0:TC],
                                    in0=u_t[:, base + 1 + kk:base + 1 + kk + TC],
                                    scalar1=convw_sb[:, dbt * 4 + kk:dbt * 4 + kk + 1],
                                    scalar2=None, op0=ALU.mult)
                                last = e.tensor_tensor(out=cv[:, 0:TC], in0=cv[:, 0:TC],
                                                       in1=tv[:, 0:TC], op=ALU.add)
                            return last.then_inc(s_dve, 1)
                        C["dve"] += 1
                        d_conv = C["dve"]
                        add("vector", convf, waits=(("act", C["act"]),))
                        if debug and k == 0 and dbt == 0:
                            add("sync", lambda e: e.dma_start(
                                out=dbg["dbg_cv"],
                                in_=sp_scr.bitcast(BF16)[:, 0:TC]),
                                waits=(("dve", d_conv),), inc="st")
                        add("scalar", lambda e, dbt=dbt: e.activation(
                            out=uc_t[:, dbt * TC:(dbt + 1) * TC],
                            in_=sp_scr.bitcast(BF16)[:, 0:TC], func=AF.Silu),
                            waits=(("dve", d_conv),), inc="act")
                else:
                    add("scalar", lambda e, et=et, half=half, p=p: e.activation(
                        out=zs_t[:, (et - 8) * TC + half * 512:
                                (et - 8) * TC + (half + 1) * 512],
                        in_=p, func=AF.Silu),
                        waits=(("pe", pe_n),), inc="act")
        ACT_XZ = C["act"]
        ACT_UC = C["act"]

        for half in range(2):
            p = ps_a if half == 0 else ps_b

            def mmf(e, half=half, p=p):
                last = None
                for kt in range(NDB):
                    last = e.matmul(out=p[0:96, :],
                                    lhsT=wxb_sb[:, kt * 96:(kt + 1) * 96],
                                    rhs=uc_t[:, kt * TC + half * 512:
                                             kt * TC + (half + 1) * 512],
                                    start=(kt == 0), stop=(kt == NDB - 1))
                return last.then_inc(s_pe, 1)
            C["pe"] += 1
            pe_n = C["pe"]
            add("tensor", mmf, waits=(("act", ACT_UC),))
            add("vector", lambda e, half=half, p=p: e.tensor_copy(
                out=xdbl_t[0:96, half * 512:(half + 1) * 512], in_=p[0:96, :]),
                waits=(("pe", pe_n),), inc="dve")
        DVE_XP = C["dve"]

        st_before = C["st"]

        def xch(e, k=k, st_before=st_before):
            rg = get_sync_regs(e)
            with e.If(rg["role"] == 0):
                e.dma_start(out=sh_xdbl[0, k, :, :], in_=xdbl_t[0:96, :]).then_inc(s_st, 16)
            with e.Else():
                e.dma_start(out=sh_xdbl[1, k, :, :], in_=xdbl_t[0:96, :]).then_inc(s_st, 16)
            nc.end_ifs()
            e.wait_ge(s_st, st_before + 16)
            with e.If(rg["role"] == 0):
                e.dma_start(out=sh_flag[0:1, k:k + 1],
                            in_=flagv_sb[0:1, k:k + 1]).then_inc(s_st, 16)
            with e.Else():
                e.dma_start(out=sh_flag[1:2, k:k + 1],
                            in_=flagv_sb[0:1, k:k + 1]).then_inc(s_st, 16)
            nc.end_ifs()
            emit_poll(e, 1, 0, k, k, 0)
            with e.If(rg["role"] == 0):
                e.dma_start(out=xdblp_t[0:96, :], in_=sh_xdbl[1, k, :, :]).then_inc(s_ld, 16)
            with e.Else():
                e.dma_start(out=xdblp_t[0:96, :], in_=sh_xdbl[0, k, :, :]).then_inc(s_ld, 16)
            nc.end_ifs()
        C["st"] += 32
        C["ld"] += 16
        add("sync", xch, waits=(("dve", DVE_XP),))
        LD_XD = C["ld"]

        def xsum(e):
            e.tensor_tensor(out=xdbl_t[0:96, :], in0=xdbl_t[0:96, :],
                            in1=xdblp_t[0:96, :], op=ALU.add)
            e.tensor_copy(out=dtr_b, in_=xdbl_t[0:64, :])
            return e.tensor_copy(out=bc_b, in_=xdbl_t[64:96, :])
        DVE_XS = add("vector", xsum, waits=(("ld", LD_XD),), inc="dve")

        if debug and k == 0:
            add("sync", lambda e: e.dma_start(out=dbg["dbg_xln"], in_=bigscr[:, 16 * TC:24 * TC]),
                waits=(("act", ACT_LN),), inc="st")
            add("sync", lambda e: e.dma_start(out=dbg["dbg_u"], in_=u_t),
                waits=(("act", ACT_XZ),), inc="st")
            add("sync", lambda e: e.dma_start(out=dbg["dbg_zs"], in_=zs_t),
                waits=(("act", ACT_XZ),), inc="st")
            add("sync", lambda e: e.dma_start(out=dbg["dbg_uc"], in_=uc_t),
                waits=(("act", ACT_UC),), inc="st")
            add("sync", lambda e: e.dma_start(out=dbg["dbg_xdbl"][0:96, :], in_=xdbl_t[0:96, :]),
                waits=(("dve", DVE_XS),), inc="st")
            add("sync", lambda e: e.dma_start(out=dbg["dbg_m"], in_=m_sb),
                waits=(("act", ACT_M),), inc="st")

        rep_dve_hist = {}
        for s in range(32):
            dst = brep(s) if s < 16 else crep(s - 16)
            pA, pB = (ps_a, ps_b) if s % 2 == 0 else (ps_q, ps_r)

            def mmf(e, s=s, pA=pA, pB=pB):
                e.matmul(out=pA, lhsT=selrep[:, s * 128:(s + 1) * 128],
                         rhs=bc_b[0:32, 0:512], start=True, stop=True)
                return e.matmul(out=pB, lhsT=selrep[:, s * 128:(s + 1) * 128],
                                rhs=bc_b[0:32, 512:1024], start=True, stop=True
                                ).then_inc(s_pe, 1)
            C["pe"] += 1
            pe_n = C["pe"]
            wm = [("dve", DVE_XS if s < 2 else rep_dve_hist[s - 2])]
            add("tensor", mmf, waits=tuple(wm))

            def evacf(e, dst=dst, pA=pA, pB=pB):
                e.tensor_copy(out=dst[:, 0:512], in_=pA)
                return e.tensor_copy(out=dst[:, 512:1024], in_=pB)
            rep_dve_hist[s] = add("vector", evacf, waits=(("pe", pe_n),), inc="dve")
        DVE_REP = C["dve"]

        for dbt in range(NDB):
            def dtmm(e, dbt=dbt):
                e.matmul(out=ps_q, lhsT=wdtb_sb[:, dbt * 128:(dbt + 1) * 128],
                         rhs=dtr_b[:, 0:512], start=True, stop=True)
                return e.matmul(out=ps_r, lhsT=wdtb_sb[:, dbt * 128:(dbt + 1) * 128],
                                rhs=dtr_b[:, 512:1024], start=True, stop=True
                                ).then_inc(s_pe, 1)
            C["pe"] += 1
            pe_dt = C["pe"]
            add("tensor", dtmm, waits=(("dve", C["dve"]),))

            def sp1(e, dbt=dbt):
                e.tensor_scalar(out=sg_scr[:, 0:512], in0=ps_q,
                                scalar1=bdt_sb[:, dbt:dbt + 1], scalar2=None,
                                op0=ALU.add)
                e.tensor_scalar(out=sg_scr[:, 512:1024], in0=ps_r,
                                scalar1=bdt_sb[:, dbt:dbt + 1], scalar2=None,
                                op0=ALU.add)
                return e.tensor_scalar_min(out=sp_scr, in0=sg_scr, scalar1=15.0)
            d_sp1 = add("vector", sp1, waits=(("pe", pe_dt),), inc="dve")
            a_sg = add("scalar", lambda e: e.activation(out=sp_scr, in_=sp_scr,
                                                        func=AF.Sigmoid, scale=-1.0),
                       waits=(("dve", d_sp1),), inc="act")
            a_ln = add("scalar", lambda e: e.activation(out=sp_scr, in_=sp_scr,
                                                        func=AF.Ln), inc="act")

            def sp2(e, dbt=dbt):
                e.scalar_tensor_tensor(out=dtb_t, in0=sp_scr,
                                       scalar=-1.0, in1=sg_scr,
                                       op0=ALU.mult, op1=ALU.max)
                return e.tensor_tensor(out=dtu_t, in0=dtb_t,
                                       in1=uc_t[:, dbt * TC:(dbt + 1) * TC], op=ALU.mult)
            DVE_DT = add("vector", sp2, waits=(("act", a_ln),), inc="dve")
            if debug and k == 0 and dbt == 0:
                add("sync", lambda e: e.dma_start(out=dbg["dbg_dt"], in_=dtb_t),
                    waits=(("dve", DVE_DT),), inc="st")
                add("sync", lambda e: e.dma_start(out=dbg["dbg_dtu"], in_=dtu_t),
                    waits=(("dve", DVE_DT),), inc="st")
                add("sync", lambda e: e.dma_start(out=dbg["dbg_brep"], in_=brep(0)),
                    waits=(("dve", DVE_DT),), inc="st")
                add("sync", lambda e: e.dma_start(out=dbg["dbg_crep"], in_=crep(0)),
                    waits=(("dve", DVE_DT),), inc="st")

            dve_hist = {}
            pe_hist = {}
            for s in range(S):
                pb = s % 2
                sl = slice(pb * TC, (pb + 1) * TC)
                wda = [("dve", DVE_DT)]
                if s >= 2:
                    wda.append(("dve", dve_hist[s - 2]))
                a_da = add("scalar", lambda e, dbt=dbt, s=s, sl=sl: e.activation(
                    out=dA_t[:, sl], in_=dtb_t, func=AF.Exp,
                    scale=aneg_sb[:, dbt * S + s:dbt * S + s + 1]),
                    waits=tuple(wda), inc="act")

                def sloop(e, dbt=dbt, s=s, sl=sl):
                    e.tensor_tensor(out=b_t, in0=dtu_t, in1=brep(s), op=ALU.mult)
                    idx = dbt * S + s
                    e.tensor_tensor_scan(out=h_t, data0=dA_t[:, sl], data1=b_t,
                                         initial=hstate[:, idx:idx + 1],
                                         op0=ALU.mult, op1=ALU.add)
                    e.tensor_tensor(out=mm_t[:, sl], in0=h_t,
                                    in1=crep(s), op=ALU.mult)
                    return e.tensor_copy(out=hstate[:, idx:idx + 1],
                                         in_=h_t[:, TC - 1:TC])
                wsl = [("act", a_da)]
                if s >= 2:
                    wsl.append(("pe", pe_hist[s - 2]))
                d_s = add("vector", sloop, waits=tuple(wsl), inc="dve")
                dve_hist[s] = d_s

                def ymm(e, s=s, lo=pb * TC):
                    e.matmul(out=ps_y0, lhsT=ident, rhs=mm_t[:, lo:lo + 512],
                             start=(s == 0), stop=(s == S - 1))
                    return e.matmul(out=ps_y1, lhsT=ident,
                                    rhs=mm_t[:, lo + 512:lo + 1024],
                                    start=(s == 0), stop=(s == S - 1)
                                    ).then_inc(s_pe, 1)
                C["pe"] += 1
                pe_hist[s] = C["pe"]
                add("tensor", ymm, waits=(("dve", d_s),))
            PE_YS = C["pe"]

            yb = dbt % 2

            def yfin(e, dbt=dbt, yb=yb):
                e.scalar_tensor_tensor(out=yfin_t[:, yb * TC:yb * TC + 512],
                                       in0=uc_t[:, dbt * TC:dbt * TC + 512],
                                       scalar=dvec_sb[:, dbt:dbt + 1], in1=ps_y0,
                                       op0=ALU.mult, op1=ALU.add)
                e.scalar_tensor_tensor(out=yfin_t[:, yb * TC + 512:(yb + 1) * TC],
                                       in0=uc_t[:, dbt * TC + 512:(dbt + 1) * TC],
                                       scalar=dvec_sb[:, dbt:dbt + 1], in1=ps_y1,
                                       op0=ALU.mult, op1=ALU.add)
                return e.tensor_tensor(out=yfin_t[:, yb * TC:(yb + 1) * TC],
                                       in0=yfin_t[:, yb * TC:(yb + 1) * TC],
                                       in1=zs_t[:, dbt * TC:(dbt + 1) * TC], op=ALU.mult)
            d_yf = add("vector", yfin,
                       waits=(("pe", PE_YS), ("st", C["st"] - 16 if dbt >= 2 else 0)),
                       inc="dve")

            def ystore(e, k=k, yb=yb, dbt=dbt):
                rg = get_sync_regs(e)
                with e.If(rg["role"] == 0):
                    e.dma_start(out=sh_y[0, k, dbt * 128:(dbt + 1) * 128, :],
                                in_=yfin_t[:, yb * TC:(yb + 1) * TC]).then_inc(s_st, 16)
                with e.Else():
                    e.dma_start(out=sh_y[1, k, dbt * 128:(dbt + 1) * 128, :],
                                in_=yfin_t[:, yb * TC:(yb + 1) * TC]).then_inc(s_st, 16)
                nc.end_ifs()
            C["st"] += 16
            add("sync", ystore, waits=(("dve", d_yf),))
            if debug and k == 0 and dbt == 0:
                add("sync", lambda e: e.dma_start(out=dbg["dbg_yfin"],
                                                  in_=yfin_t[:, 0:TC]),
                    waits=(("dve", d_yf),), inc="st")
                add("sync", lambda e: e.dma_start(out=dbg["dbg_hst"], in_=hstate),
                    waits=(("dve", d_yf),), inc="st")
        ST_Y = C["st"]

        def yflag(e, k=k, ST_Y=ST_Y):
            rg = get_sync_regs(e)
            e.wait_ge(s_st, ST_Y)
            with e.If(rg["role"] == 0):
                e.dma_start(out=sh_flag[2:3, k:k + 1],
                            in_=flagv_sb[0:1, NCH + k:NCH + k + 1]).then_inc(s_st, 16)
            with e.Else():
                e.dma_start(out=sh_flag[3:4, k:k + 1],
                            in_=flagv_sb[0:1, NCH + k:NCH + k + 1]).then_inc(s_st, 16)
            nc.end_ifs()
        C["st"] += 16
        add("sync", yflag)
        ST_YF = C["st"]

        ld0, st0, pe0, act0, dve0 = C["ld"], C["st"], C["pe"], C["act"], C["dve"]
        sc_parts, pc_parts, ac_parts, vc_parts = [], [], [], []

        for th in range(2):
            pe_attn_groups = []
            dve_banks_free = C["dve"]
            act_banks_free = C["act"]
            for kt in range(16):
                wb2 = kt % 2
                yb2 = kt % 2
                w_pe = pe_attn_groups[kt - 2] if kt >= 2 else 0
                C["ld"] += 16
                ld_w = C["ld"]

                def ldw(e, kt=kt, wb2=wb2, w_pe=w_pe):
                    if w_pe:
                        e.wait_ge(s_pe, w_pe)
                    e.dma_start(out=wout_blk[:, wb2 * DM:(wb2 + 1) * DM],
                                in_=wout_b[kt * 128:(kt + 1) * 128, :]).then_inc(s_ld, 16)
                sc_parts.append(ldw)
                C["ld"] += 16
                ld_y = C["ld"]

                def ldy(e, kt=kt, yb2=yb2, k=k, th=th):
                    e.dma_start(out=ya_t[:, yb2 * 512:(yb2 + 1) * 512],
                                in_=sh_y[kt // 8, k, (kt % 8) * 128:(kt % 8 + 1) * 128,
                                         th * 512:(th + 1) * 512]).then_inc(s_ld, 16)
                sc_parts.append(ldy)
                C["pe"] += 1
                pe_attn_groups.append(C["pe"])

                def attnmm(e, kt=kt, wb2=wb2, yb2=yb2, ld_y=ld_y,
                           dbf=dve_banks_free):
                    e.wait_ge(s_ld, ld_y)
                    if kt == 0:
                        e.wait_ge(s_dve, dbf)
                    last = None
                    for et in range(8):
                        last = e.matmul(out=banks[et],
                                        lhsT=wout_blk[:, wb2 * DM + et * 128:
                                                      wb2 * DM + (et + 1) * 128],
                                        rhs=ya_t[:, yb2 * 512:(yb2 + 1) * 512],
                                        start=(kt == 0), stop=False)
                    last.then_inc(s_pe, 1)
                pc_parts.append(attnmm)

            for et in range(8):
                C["ld"] += 16
                ld_x = C["ld"]
                dve_prev = C["dve"]

                def ldx(e, et=et, k=k, th=th, w=dve_prev if et >= 1 else 0):
                    if w:
                        e.wait_ge(s_dve, w)
                    e.dma_start(out=xfm_t,
                                in_=xbt[et * 128:(et + 1) * 128,
                                        k * TC + th * 512:k * TC + (th + 1) * 512]
                                ).then_inc(s_ld, 16)
                sc_parts.append(ldx)
                C["dve"] += 1
                d_xc = C["dve"]

                def xcast(e, et=et, ld_x=ld_x):
                    e.wait_ge(s_ld, ld_x)
                    e.tensor_copy(out=xfm_b[:, (et % 2) * 512:(et % 2 + 1) * 512],
                                  in_=xfm_t).then_inc(s_dve, 1)
                vc_parts.append(xcast)
                C["pe"] += 1

                def xacc(e, et=et, d=d_xc):
                    e.wait_ge(s_dve, d)
                    e.matmul(out=banks[et], lhsT=ident,
                             rhs=xfm_b[:, (et % 2) * 512:(et % 2 + 1) * 512],
                             start=False, stop=True).then_inc(s_pe, 1)
                pc_parts.append(xacc)
            PE_ATTN = C["pe"]

            for et in range(8):
                C["act"] += 2

                def x1cast(e, et=et, PE_ATTN=PE_ATTN):
                    e.wait_ge(s_pe, PE_ATTN)
                    e.activation(out=x1b(et), in_=banks[et],
                                 func=AF.Copy).then_inc(s_act, 1)
                    e.activation(out=x1sq(et), in_=banks[et],
                                 func=AF.Square).then_inc(s_act, 1)
                ac_parts.append(x1cast)
            ACT_X1 = C["act"]

            C["pe"] += 1
            pe_stats = C["pe"]

            def statsmm(e, ACT_X1=ACT_X1):
                e.wait_ge(s_act, ACT_X1)
                for et in range(8):
                    e.matmul(out=ps_st[0:1, :], lhsT=ones_col, rhs=x1b(et),
                             start=(et == 0), stop=(et == 7))
                last = None
                for et in range(8):
                    last = e.matmul(out=ps_q[0:1, :], lhsT=ones_col, rhs=x1sq(et),
                                    start=(et == 0), stop=(et == 7))
                last.then_inc(s_pe, 1)
            pc_parts.append(statsmm)

            C["dve"] += 1
            d_st1 = C["dve"]

            def statsv(e, pe_stats=pe_stats):
                e.wait_ge(s_pe, pe_stats)
                e.tensor_scalar_mul(out=musig[0:1, 0:512], in0=ps_st[0:1, :],
                                    scalar1=1.0 / DM)
                e.tensor_scalar_mul(out=musig[0:1, 512:1024], in0=ps_q[0:1, :],
                                    scalar1=1.0 / DM)
                e.tensor_tensor(out=musig[0:1, 1024:1536], in0=musig[0:1, 0:512],
                                in1=musig[0:1, 0:512], op=ALU.mult)
                e.tensor_tensor(out=musig[0:1, 1024:1536], in0=musig[0:1, 512:1024],
                                in1=musig[0:1, 1024:1536],
                                op=ALU.subtract).then_inc(s_dve, 1)
            vc_parts.append(statsv)
            C["act"] += 1
            a_sq = C["act"]

            def statsa(e, d=d_st1):
                e.wait_ge(s_dve, d)
                e.activation(out=musig[0:1, 512:1024], in_=musig[0:1, 1024:1536],
                             func=AF.Sqrt, bias=eps_t[0:1, :]).then_inc(s_act, 1)
            ac_parts.append(statsa)
            C["dve"] += 1
            d_st2 = C["dve"]

            def statsv2(e, a=a_sq):
                e.wait_ge(s_act, a)
                e.reciprocal(out=musig[0:1, 1024:1536], in_=musig[0:1, 512:1024])
                e.tensor_scalar_mul(out=musig[0:1, 512:1024], in0=musig[0:1, 1024:1536],
                                    scalar1=-1.0)
                e.tensor_copy(out=rowbf[0:1, 0:512], in_=musig[0:1, 512:1024])
                e.tensor_copy(out=rowbf[0:1, 512:1024],
                              in_=musig[0:1, 0:512]).then_inc(s_dve, 1)
            vc_parts.append(statsv2)
            C["pe"] += 1
            pe_rep = C["pe"]

            def repmm(e, d=d_st2):
                e.wait_ge(s_dve, d)
                e.matmul(out=ps_a, lhsT=ones_row, rhs=rowbf[0:1, 0:512],
                         start=True, stop=True)
                e.matmul(out=ps_b, lhsT=ones_row, rhs=rowbf[0:1, 512:1024],
                         start=True, stop=True).then_inc(s_pe, 1)
            pc_parts.append(repmm)
            C["dve"] += 1
            d_rep = C["dve"]

            def repev(e, p=pe_rep):
                e.wait_ge(s_pe, p)
                e.tensor_copy(out=s2rep, in_=ps_a)
                e.tensor_copy(out=mu2rep, in_=ps_b).then_inc(s_dve, 1)
            vc_parts.append(repev)

            d_fin_prev = {}
            for ot in range(8):
                pbank = banks[ot % 2]
                w_fin = d_fin_prev.get(ot - 2, d_rep)
                C["pe"] += 1
                pe_ffn = C["pe"]

                def ffnmm(e, ot=ot, pbank=pbank, w=w_fin, a=ACT_X1):
                    e.wait_ge(s_dve, w)
                    e.wait_ge(s_act, a)
                    last = None
                    for kt in range(8):
                        last = e.matmul(out=pbank,
                                        lhsT=m_sb[:, kt * DM + ot * 128:
                                                  kt * DM + (ot + 1) * 128],
                                        rhs=x1b(kt), start=(kt == 0), stop=(kt == 7))
                    last.then_inc(s_pe, 1)
                pc_parts.append(ffnmm)
                C["dve"] += 1
                d_fin = C["dve"]
                d_fin_prev[ot] = d_fin
                st_w = C["st"] if (ot > 0 or th > 0) else 0

                def ffnfin(e, ot=ot, pbank=pbank, p=pe_ffn, st_w=st_w):
                    e.wait_ge(s_pe, p)
                    if st_w:
                        e.wait_ge(s_st, st_w)
                    e.scalar_tensor_tensor(out=cfin_b, in0=mu2rep,
                                           scalar=mbar_sb[:, ot:ot + 1], in1=pbank,
                                           op0=ALU.mult, op1=ALU.subtract)
                    e.tensor_tensor(out=cfin_b, in0=cfin_b, in1=s2rep, op=ALU.mult)
                    e.scalar_tensor_tensor(out=outet_t,
                                           in0=cfin_b, scalar=fbias_sb[:, ot:ot + 1],
                                           in1=x1b(ot), op0=ALU.add,
                                           op1=ALU.add).then_inc(s_dve, 1)
                vc_parts.append(ffnfin)
                C["st"] += 16

                def ostore(e, ot=ot, k=k, th=th, d=d_fin):
                    e.wait_ge(s_dve, d)
                    e.dma_start(out=out_d[k // 2, ot * 128:(ot + 1) * 128,
                                          th * 512:(th + 1) * 512],
                                in_=outet_t).then_inc(s_st, 16)
                sc_parts.append(ostore)

        n_ld_c = C["ld"] - ld0
        n_st_c = C["st"] - st0
        n_pe_c = C["pe"] - pe0
        n_act_c = C["act"] - act0
        n_dve_c = C["dve"] - dve0

        def sync_c(e, k=k, par=par, parts=tuple(sc_parts), n_ld=n_ld_c, n_st=n_st_c,
                   ST_YF=ST_YF):
            rg = get_sync_regs(e)
            cond = (rg["role"] == 0) if par == 0 else (rg["role"] == 1)
            with e.If(cond):
                row = 3 if par == 0 else 2
                e.reg_load(rg["tgt"], flagv_sb[0:1, NCH + k:NCH + k + 1])
                tgtv = make_scalar_value(rg["tgt"], min_val=0, max_val=(1 << 30))
                e.reg_mov(rg["fl"], 0)
                flv = make_scalar_value(rg["fl"], min_val=0, max_val=(1 << 30))
                cntv = make_scalar_value(rg["cnt"], min_val=0, max_val=(1 << 30))
                with e.While(lambda: flv != tgtv):
                    e.dma_start(out=pollbuf[0:1, 1:2],
                                in_=sh_flag[row:row + 1, k:k + 1]).then_inc(s_pol, 16)
                    e.reg_add(rg["cnt"], rg["cnt"], 16)
                    e.wait_ge(s_pol, cntv)
                    e.reg_load(rg["fl"], pollbuf[0:1, 1:2])
                e.wait_ge(s_st, ST_YF)
                for p in parts:
                    p(e)
            with e.Else():
                e.sem_inc(s_ld, n_ld)
                e.sem_inc(s_st, n_st)
            nc.end_ifs()
        add("sync", sync_c)

        def mk_wrap(parts, sem_, n, par=par, k=k):
            parts = tuple(parts)
            name = f"rw{k}_{id(parts) % 100000}"

            def f(e, name=name):
                r = e.alloc_register(name)
                e.wait_ge(s_ld, LD_PRO)
                e.reg_load(r, role_sb[0:1, 0:1])
                rv = make_scalar_value(r, min_val=0, max_val=1)
                cond = (rv == 0) if par == 0 else (rv == 1)
                with e.If(cond):
                    for p in parts:
                        p(e)
                with e.Else():
                    if n > 0:
                        e.sem_inc(sem_, n)
                nc.end_ifs()
                e.free_register(r)
            return f

        add("tensor", mk_wrap(pc_parts, s_pe, n_pe_c))
        add("scalar", mk_wrap(ac_parts, s_act, n_act_c))
        add("vector", mk_wrap(vc_parts, s_dve, n_dve_c))

    FINAL_ST = C["st"]
    add("sync", lambda e: e.wait_ge(s_st, FINAL_ST))

    with nc.Block() as block:
        @block.sync
        def _(e):
            for f in steps["sync"]:
                f(e)

        @block.tensor
        def _(e):
            for f in steps["tensor"]:
                f(e)

        @block.scalar
        def _(e):
            for f in steps["scalar"]:
                f(e)

        @block.vector
        def _(e):
            for f in steps["vector"]:
                f(e)

    for g in reversed(ctx):
        g.__exit__(None, None, None)
    return nc



def _selrep():
    m = np.zeros((32, 32 * 128), dtype=np.float32)
    for s in range(32):
        m[s, s * 128:(s + 1) * 128] = 1.0
    return m.astype(bf16)


def _dft_tables():
    n = np.arange(DM, dtype=np.float64)
    ang = 2.0 * np.pi / DM * np.outer(n, n)
    ct = np.cos(ang).astype(np.float32).astype(bf16)
    st = np.sin(ang).astype(np.float32).astype(bf16)
    return ct, st


def _per_core_inputs(inputs, c, T, nonce):
    b, j = c // 2, c % 2
    NCH = T // TC
    f32 = np.float32
    W_in = np.asarray(inputs["L0_attn_W_in"], f32)
    W_x = np.asarray(inputs["L0_attn_W_x"], f32)
    W_dt = np.asarray(inputs["L0_attn_W_dt"], f32)
    W_out = np.asarray(inputs["L0_attn_W_out"], f32)
    x = np.ascontiguousarray(np.asarray(inputs["L0_attn_x"], f32)[b, :T])
    lo, hi = j * DH, (j + 1) * DH

    def tile128(v):
        return np.ascontiguousarray(v.reshape(NDB, 128).T)

    ct, st = _dft_tables()
    sel = np.concatenate([W_in[lo:hi], W_in[DI + lo:DI + hi]], axis=0)
    alog = np.asarray(inputs["L0_attn_A_log"], f32)[lo:hi]
    alog_t = np.ascontiguousarray(
        alog.reshape(NDB, 128, S).transpose(1, 0, 2).reshape(128, NDB * S))
    convw = np.asarray(inputs["L0_attn_conv_w"], f32)[lo:hi]
    convw_t = np.ascontiguousarray(
        convw.reshape(NDB, 128, 4).transpose(1, 0, 2).reshape(128, NDB * 4))
    flagv = (np.arange(1, 2 * NCH + 1, dtype=np.uint64) + np.uint64(nonce)) % np.uint64(2**32)
    return {
        "xb": x,
        "xbt": np.ascontiguousarray(x.T),
        "win_t": np.ascontiguousarray(sel.T),
        "wx_t": np.ascontiguousarray(W_x[:, lo:hi].T),
        "wdt_t": np.ascontiguousarray(W_dt[lo:hi].T),
        "bdt_in": tile128(np.asarray(inputs["L0_attn_b_dt"], f32)[lo:hi]),
        "alog_in": alog_t,
        "convw_in": convw_t,
        "convb_in": tile128(np.asarray(inputs["L0_attn_conv_b"], f32)[lo:hi]),
        "dvec_in": tile128(np.asarray(inputs["L0_attn_D"], f32)[lo:hi]),
        "wout_t": np.ascontiguousarray(W_out.T),
        "wr_t": np.ascontiguousarray(np.asarray(inputs["L0_ffn_w_freq_real"], f32).T),
        "wi_t": np.ascontiguousarray(np.asarray(inputs["L0_ffn_w_freq_imag"], f32).T),
        "ct_in": ct,
        "st_in": st,
        "fbias_in": tile128(np.asarray(inputs["L0_ffn_bias"], f32)),
        "ident_in": np.eye(128, dtype=np.float32).astype(bf16),
        "selrep_in": _selrep(),
        "role_in": np.array([[j, 0, 0, 0]], dtype=np.uint32),
        "flagv_in": flagv.astype(np.uint32).reshape(1, -1),
    }


_CACHE = {}


def kernel(**inputs):
    T = int(np.asarray(inputs["L0_attn_x"]).shape[1])
    if T not in _CACHE:
        _CACHE[T] = build_program(T)
    nc = _CACHE[T]
    nonce = int.from_bytes(np.random.bytes(4), "little")
    in_maps = [_per_core_inputs(inputs, c, T, nonce) for c in range(8)]
    res = run_bass_kernel_spmd(nc, in_maps, core_ids=list(range(8)))
    NCH = T // TC
    out = np.empty((B, T, DM), dtype=np.float32)
    for c in range(8):
        b, j = c // 2, c % 2
        o = res.results[c]["out_d"]
        for i in range(NCH // 2):
            k = 2 * i + j
            out[b, k * TC:(k + 1) * TC, :] = o[i].T
    return out


# version 04
# speedup vs baseline: 1.3000x; 1.3000x over previous
import numpy as np
import ml_dtypes
import concourse.bass as bass
import concourse.mybir as mybir
from concourse.bass_utils import run_bass_kernel_spmd
from concourse.expressions import make_scalar_value

F32 = mybir.dt.float32
BF16 = mybir.dt.bfloat16
U32 = mybir.dt.uint32
AF = mybir.ActivationFunctionType
ALU = mybir.AluOpType
bf16 = ml_dtypes.bfloat16

B = 4
DM = 1024
DI = 2048
DH = 1024
S = 16
RK = 64
NDB = DH // 128
TC = 1024


def build_program(T, debug=False):
    NCH = T // TC
    NSUB = TC // 128
    nc = bass.Bass("TRN2", target_bir_lowering=False, detect_race_conditions=False)

    def din(name, shape, dt=F32):
        return nc.dram_tensor(name, shape, dt, kind="ExternalInput").ap()

    xb = din("xb", [T, DM])
    xbt = din("xbt", [DM, T])
    win_t = din("win_t", [DM, 2 * DH])
    wx_t = din("wx_t", [DH, 96])
    wdt_t = din("wdt_t", [RK, DH])
    bdt_in = din("bdt_in", [128, NDB])
    alog_in = din("alog_in", [128, NDB * S])
    convw_in = din("convw_in", [128, NDB * 4])
    convb_in = din("convb_in", [128, NDB])
    dvec_in = din("dvec_in", [128, NDB])
    wout_t = din("wout_t", [DI, DM])
    wr_t = din("wr_t", [DM, DM])
    wi_t = din("wi_t", [DM, DM])
    ct_in = din("ct_in", [DM, DM], BF16)
    st_in = din("st_in", [DM, DM], BF16)
    fbias_in = din("fbias_in", [128, NDB])
    ident_in = din("ident_in", [128, 128], BF16)
    selrep_in = din("selrep_in", [32, 32 * 128], BF16)
    role_in = din("role_in", [1, 4], U32)
    flagv_in = din("flagv_in", [1, 2 * NCH], U32)

    out_d = nc.dram_tensor("out_d", [NCH // 2, DM, TC], F32, kind="ExternalOutput").ap()

    dbg = {}
    if debug:
        for nm, shape, dt_ in [("dbg_xln", [128, 8 * TC], BF16),
                               ("dbg_uc", [128, NDB * TC], BF16),
                               ("dbg_zs", [128, NDB * TC], BF16),
                               ("dbg_u", [128, NDB * (TC + 4)], BF16),
                               ("dbg_xdbl", [128, TC], F32),
                               ("dbg_dt", [128, TC], BF16),
                               ("dbg_dtu", [128, TC], BF16),
                               ("dbg_brep", [128, TC], BF16),
                               ("dbg_crep", [128, TC], BF16),
                               ("dbg_yfin", [128, TC], BF16),
                               ("dbg_m", [128, NDB * DM], BF16),
                               ("dbg_hst", [128, 128], F32),
                               ("dbg_cv", [128, TC], BF16),
                               ("dbg_xtm", [128, TC], BF16),
                               ("dbg_stat", [128, 8], F32)]:
            dbg[nm] = nc.dram_tensor(nm, shape, dt_, kind="ExternalOutput").ap()
    win_b = nc.dram_tensor("win_b", [DM, 2 * DH], BF16, kind="Internal").ap()
    wout_b = nc.dram_tensor("wout_b", [DI, DM], BF16, kind="Internal").ap()

    sh_xdbl = nc.dram_tensor("sh_xdbl", [2, NCH, 96, TC], F32, kind="Internal",
                             addr_space="Shared").ap()
    sh_y = nc.dram_tensor("sh_y", [2, NCH, DH, TC], BF16, kind="Internal",
                          addr_space="Shared").ap()
    sh_flag = nc.dram_tensor("sh_flag", [4, NCH], U32, kind="Internal",
                             addr_space="Shared").ap()

    ctx = []

    def sbuf(name, shape, dtype):
        g = nc.sbuf_tensor(name, shape, dtype)
        t = g.__enter__()
        ctx.append(g)
        return t[:, :] if hasattr(t, "__getitem__") else t

    def psum(name, shape):
        g = nc.psum_tensor(name, shape, F32)
        t = g.__enter__()
        ctx.append(g)
        return t[:, :]

    def mksem(name, num=None):
        g = nc.semaphore(name, num=num)
        t = g.__enter__()
        ctx.append(g)
        return t

    bigscr = sbuf("bigscr", [128, 32 * TC], BF16)
    u_t = sbuf("u_t", [128, NDB * (TC + 4)], BF16)
    zs_t = sbuf("zs_t", [128, NDB * TC], BF16)
    uc_t = sbuf("uc_t", [128, NDB * TC], BF16)
    m_sb = sbuf("m_sb", [128, NDB * DM], BF16)
    dA_t = sbuf("dA_t", [128, 2 * TC], BF16)
    b_t = sbuf("b_t", [128, TC], BF16)
    h_t = sbuf("h_t", [128, TC], BF16)
    mm_t = sbuf("mm_t", [128, 2 * TC], BF16)
    yfin_t = sbuf("yfin_t", [128, 2 * TC], BF16)
    dtb_t = sbuf("dtb_t", [128, TC], BF16)
    dtu_t = sbuf("dtu_t", [128, TC], BF16)
    sg_scr = sbuf("sg_scr", [128, TC], F32)
    sp_scr = sbuf("sp_scr", [128, TC], F32)
    cfin_b = sbuf("cfin_b", [128, 512], BF16)
    x_sub = sbuf("x_sub", [128, DM], F32)
    xdbl_t = sbuf("xdbl_t", [128, TC], F32)
    dtr_b = sbuf("dtr_b", [64, TC], BF16)
    bc_b = sbuf("bc_b", [32, TC], BF16)
    win_blk = sbuf("win_blk", [128, 2 * NDB * 128], BF16)
    wout_blk = sbuf("wout_blk", [128, 2 * DM], BF16)
    ya_t = sbuf("ya_t", [128, 2 * 512], BF16)
    xfm_t = sbuf("xfm_t", [128, 512], F32)
    xfm_b = sbuf("xfm_b", [128, 2 * 512], BF16)
    outet_t = sbuf("outet_t", [128, 512], F32)
    ident = sbuf("ident", [128, 128], BF16)
    ones_col = sbuf("ones_col", [128, 1], BF16)
    ones_row = sbuf("ones_row", [1, 128], BF16)
    bdt_sb = sbuf("bdt_sb", [128, NDB], F32)
    aneg_sb = sbuf("aneg_sb", [128, NDB * S], F32)
    convw_sb = sbuf("convw_sb", [128, NDB * 4], F32)
    convb_sb = sbuf("convb_sb", [128, NDB], F32)
    dvec_sb = sbuf("dvec_sb", [128, NDB], F32)
    fbias_sb = sbuf("fbias_sb", [128, NDB], F32)
    role_sb = sbuf("role_sb", [1, 4], U32)
    flagv_sb = sbuf("flagv_sb", [1, 2 * NCH], U32)
    pollbuf = sbuf("pollbuf", [1, 4], U32)
    hstate = sbuf("hstate", [128, 128], F32)
    wxb_sb = sbuf("wxb_sb", [128, NDB * 96], BF16)
    selrep = sbuf("selrep", [32, 32 * 128], BF16)
    wdtb_sb = sbuf("wdtb_sb", [64, DH], BF16)
    mbar_sb = sbuf("mbar_sb", [128, NDB], F32)
    stat_s = sbuf("stat_s", [128, 8], F32)
    eps_t = sbuf("eps_t", [128, 1], F32)
    musig = sbuf("musig", [1, 3 * 512], F32)
    rowbf = sbuf("rowbf", [1, 2 * 512], BF16)

    xdblp_t = None

    def brep(s):
        return bigscr[:, s * TC:(s + 1) * TC]

    def crep(s):
        return bigscr[:, (16 + s) * TC:(17 + s) * TC]

    def xlnf(kt):
        return bigscr[:, (16 + kt) * TC:(17 + kt) * TC]

    xln_tm = bigscr[:, 0:TC]
    xdblp_t = bigscr[:, 0:2 * TC].bitcast(F32)
    sqtmp = bigscr[:, TC:2 * TC]
    castbuf = zs_t[:, 0:4 * TC].bitcast(F32)
    castout = u_t[:, 0:2 * TC]

    def wrb(kt):
        return uc_t[:, kt * DM:(kt + 1) * DM]

    def wib(kt):
        return bigscr[:, (16 + kt) * DM:(17 + kt) * DM]

    def ctb(kt):
        return zs_t[:, kt * DM:(kt + 1) * DM]

    def stbl(kt):
        return u_t[:, kt * (TC + 4):kt * (TC + 4) + DM]

    def Ureg(kt):
        return bigscr[:, kt * DM:(kt + 1) * DM]

    def Vreg(kt):
        return bigscr[:, (24 + kt) * DM:(25 + kt) * DM]

    big32 = bigscr.bitcast(F32)

    def x1f(et):
        return big32[:, et * 512:(et + 1) * 512]

    def x1b(et):
        return bigscr[:, 8 * TC + et * 512:8 * TC + (et + 1) * 512]

    def x1sq(et):
        return bigscr[:, 12 * TC + et * 512:12 * TC + (et + 1) * 512]

    s2rep = bigscr[:, 16 * TC:16 * TC + 512]
    mu2rep = bigscr[:, 16 * TC + 512:17 * TC]

    banks = [psum(f"psb{i}", [128, 512]) for i in range(8)]
    ps_tp = banks[0]
    ps_a = banks[1]
    ps_b = banks[2]
    ps_y0 = banks[3]
    ps_y1 = banks[4]
    ps_st = banks[5]
    ps_q = banks[6]
    ps_r = banks[7]

    s_ld = mksem("s_ld")
    s_st = mksem("s_st")
    s_pe = mksem("s_pe")
    s_act = mksem("s_act")
    s_dve = mksem("s_dve")
    s_pol = mksem("s_pol", num=200)
    SEMS = {"ld": s_ld, "st": s_st, "pe": s_pe, "act": s_act, "dve": s_dve}
    C = {"ld": 0, "st": 0, "pe": 0, "act": 0, "dve": 0}
    steps = {"sync": [], "tensor": [], "scalar": [], "vector": []}

    def add(eng, body, waits=(), inc=None):
        wl = tuple((SEMS[n], v) for (n, v) in waits if v > 0)
        if inc is None:
            def run(e, wl=wl, body=body):
                for sem_, v in wl:
                    e.wait_ge(sem_, v)
                body(e)
            steps[eng].append(run)
            return None
        amt = 16 if inc in ("ld", "st") else 1
        C[inc] += amt
        sem_i = SEMS[inc]

        def run(e, wl=wl, body=body, sem_i=sem_i, amt=amt):
            for sem_, v in wl:
                e.wait_ge(sem_, v)
            inst = body(e)
            inst.then_inc(sem_i, amt)
        steps[eng].append(run)
        return C[inc]

    for dst, src in [(ident, ident_in), (selrep, selrep_in), (bdt_sb, bdt_in), (aneg_sb, alog_in),
                     (convw_sb, convw_in), (convb_sb, convb_in), (dvec_sb, dvec_in),
                     (fbias_sb, fbias_in), (role_sb, role_in), (flagv_sb, flagv_in)]:
        add("sync", lambda e, d=dst, s=src: e.dma_start(out=d, in_=s), inc="ld")
    LD_PRO = C["ld"]

    add("scalar", lambda e: e.activation(out=aneg_sb, in_=aneg_sb, func=AF.Exp),
        waits=(("ld", LD_PRO),), inc="act")
    ACT_PRO = C["act"]

    def dve_pro(e):
        e.tensor_scalar_mul(out=aneg_sb, in0=aneg_sb, scalar1=-1.0)
        e.memset(ones_col, 1.0)
        e.memset(ones_row, 1.0)
        e.memset(hstate, 0.0)
        e.memset(eps_t, 1e-5)
        return e.memset(pollbuf, 0)
    add("vector", dve_pro, waits=(("act", ACT_PRO), ("ld", LD_PRO)), inc="dve")

    cast_rounds = ([("win", i) for i in range(8)] + [("wout", i) for i in range(16)]
                   + [("wx", i) for i in range(NDB)] + [("wr", i) for i in range(NDB)]
                   + [("wi", i) for i in range(NDB)] + [("wdt", 0)])
    for kind, i in cast_rounds:
        if kind == "win":
            src, cols = win_t[i * 128:(i + 1) * 128, :], 2 * DH
        elif kind == "wout":
            src, cols = wout_t[i * 128:(i + 1) * 128, :], DM
        elif kind == "wx":
            src, cols = wx_t[i * 128:(i + 1) * 128, :], 96
        elif kind == "wr":
            src, cols = wr_t[i * 128:(i + 1) * 128, :], DM
        elif kind == "wdt":
            src, cols = wdt_t, DH
        else:
            src, cols = wi_t[i * 128:(i + 1) * 128, :], DM
        dst_cb = castbuf[0:64, 0:DH] if kind == "wdt" else castbuf[:, 0:cols]
        ldn = add("sync", lambda e, s=src, d=dst_cb: e.dma_start(out=d, in_=s),
                  waits=(("dve", C["dve"]), ("st", C["st"])), inc="ld")
        if kind == "win":
            dv = add("vector", lambda e: e.tensor_copy(out=castout[:, 0:2 * DH],
                                                       in_=castbuf[:, 0:2 * DH]),
                     waits=(("ld", ldn),), inc="dve")
            add("sync", lambda e, i=i: e.dma_start(out=win_b[i * 128:(i + 1) * 128, :],
                                                   in_=castout[:, 0:2 * DH]),
                waits=(("dve", dv),), inc="st")
        elif kind == "wout":
            dv = add("vector", lambda e: e.tensor_copy(out=castout[:, 0:DM],
                                                       in_=castbuf[:, 0:DM]),
                     waits=(("ld", ldn),), inc="dve")
            add("sync", lambda e, i=i: e.dma_start(out=wout_b[i * 128:(i + 1) * 128, :],
                                                   in_=castout[:, 0:DM]),
                waits=(("dve", dv),), inc="st")
        elif kind == "wx":
            add("vector", lambda e, i=i: e.tensor_copy(out=wxb_sb[:, i * 96:(i + 1) * 96],
                                                       in_=castbuf[:, 0:96]),
                waits=(("ld", ldn),), inc="dve")
        elif kind == "wdt":
            add("vector", lambda e: e.tensor_copy(out=wdtb_sb,
                                                  in_=castbuf[0:64, 0:DH]),
                waits=(("ld", ldn),), inc="dve")
        elif kind == "wr":
            add("vector", lambda e, i=i: e.tensor_copy(out=wrb(i), in_=castbuf[:, 0:DM]),
                waits=(("ld", ldn),), inc="dve")
        else:
            add("vector", lambda e, i=i: e.tensor_copy(out=wib(i), in_=castbuf[:, 0:DM]),
                waits=(("ld", ldn),), inc="dve")
    DVE_CAST = C["dve"]

    for kt in range(NDB):
        add("sync", lambda e, kt=kt: e.dma_start(out=ctb(kt),
                                                 in_=ct_in[kt * 128:(kt + 1) * 128, :]),
            waits=(("dve", DVE_CAST),), inc="ld")
        add("sync", lambda e, kt=kt: e.dma_start(out=stbl(kt),
                                                 in_=st_in[kt * 128:(kt + 1) * 128, :]),
            inc="ld")
    LD_TBL = C["ld"]

    for which in range(2):
        for rt in range(NDB):
            for half in range(2):
                cs = slice(half * 512, (half + 1) * 512)
                if which == 0:
                    def mmf(e, rt=rt, cs=cs):
                        for kt in range(NDB):
                            e.matmul(out=ps_a, lhsT=wrb(kt)[:, rt * 128:(rt + 1) * 128],
                                     rhs=ctb(kt)[:, cs], start=(kt == 0),
                                     stop=(kt == NDB - 1))
                        last = None
                        for kt in range(NDB):
                            last = e.matmul(out=ps_b,
                                            lhsT=wib(kt)[:, rt * 128:(rt + 1) * 128],
                                            rhs=stbl(kt)[:, cs], start=(kt == 0),
                                            stop=(kt == NDB - 1))
                        return last.then_inc(s_pe, 1)
                else:
                    def mmf(e, rt=rt, cs=cs):
                        n = 0
                        last = None
                        for lf, rf in [(wib, ctb), (wrb, stbl)]:
                            for kt in range(NDB):
                                n += 1
                                last = e.matmul(out=ps_a,
                                                lhsT=lf(kt)[:, rt * 128:(rt + 1) * 128],
                                                rhs=rf(kt)[:, cs], start=(n == 1),
                                                stop=(n == 16))
                        return last.then_inc(s_pe, 1)
                C["pe"] += 1
                pe_n = C["pe"]
                add("tensor", mmf, waits=(("ld", LD_TBL), ("dve", C["dve"])))
                if which == 0:
                    def uevac(e, rt=rt, half=half):
                        e.tensor_copy(out=cfin_b, in_=ps_b)
                        return e.tensor_tensor(
                            out=Ureg(rt)[:, half * 512:(half + 1) * 512],
                            in0=ps_a, in1=cfin_b, op=ALU.subtract)
                    add("vector", uevac, waits=(("pe", pe_n),), inc="dve")
                else:
                    add("vector", lambda e, rt=rt, half=half: e.tensor_copy(
                        out=Vreg(rt)[:, half * 512:(half + 1) * 512], in_=ps_a),
                        waits=(("pe", pe_n),), inc="dve")
    DVE_UV = C["dve"]

    for nt in range(NDB):
        for half in range(2):
            cs = slice(half * 512, (half + 1) * 512)

            def mmf(e, nt=nt, cs=cs):
                n = 0
                last = None
                for lf, rf in [(ctb, Ureg), (stbl, Vreg)]:
                    for rt in range(NDB):
                        n += 1
                        last = e.matmul(out=ps_a, lhsT=lf(rt)[:, nt * 128:(nt + 1) * 128],
                                        rhs=rf(rt)[:, cs], start=(n == 1), stop=(n == 16))
                return last.then_inc(s_pe, 1)
            C["pe"] += 1
            pe_n = C["pe"]
            add("tensor", mmf, waits=(("dve", DVE_UV), ("act", C["act"])))
            add("scalar", lambda e, nt=nt, half=half: e.activation(
                out=m_sb[:, nt * DM + half * 512:nt * DM + (half + 1) * 512],
                in_=ps_a, func=AF.Copy, scale=1.0 / DM),
                waits=(("pe", pe_n),), inc="act")
    ACT_M = C["act"]

    for ot in range(NDB):
        def mmf(e, ot=ot):
            last = None
            for nt in range(NDB):
                last = e.matmul(out=ps_st[:, 0:1],
                                lhsT=m_sb[:, nt * DM + ot * 128:nt * DM + (ot + 1) * 128],
                                rhs=ones_col, start=(nt == 0), stop=(nt == NDB - 1))
            return last.then_inc(s_pe, 1)
        C["pe"] += 1
        pe_n = C["pe"]
        add("tensor", mmf, waits=(("act", ACT_M), ("dve", C["dve"])))
        add("vector", lambda e, ot=ot: e.tensor_copy(out=mbar_sb[:, ot:ot + 1],
                                                     in_=ps_st[:, 0:1]),
            waits=(("pe", pe_n),), inc="dve")
    PE_M = C["pe"]

    def clr(e):
        last = None
        for dbt in range(NDB):
            last = e.memset(u_t[:, dbt * (TC + 4):dbt * (TC + 4) + 4], 0.0)
        return last.then_inc(s_dve, 1)
    C["dve"] += 1
    add("vector", clr, waits=(("pe", PE_M),))
    DVE_CLR = C["dve"]

    sync_regs = {}

    def get_sync_regs(e):
        if "role" not in sync_regs:
            e.wait_ge(s_ld, LD_PRO)
            r = e.alloc_register("r_role")
            e.reg_load(r, role_sb[0:1, 0:1])
            sync_regs["role"] = make_scalar_value(r, min_val=0, max_val=1)
            sync_regs["fl"] = e.alloc_register("r_fl")
            sync_regs["cnt"] = e.alloc_register("r_cnt")
            e.reg_mov(sync_regs["cnt"], 0)
            sync_regs["tgt"] = e.alloc_register("r_tgt")
        return sync_regs

    def emit_poll(e, row_if, row_else, col, fvcol, slot):
        rg = get_sync_regs(e)
        e.reg_load(rg["tgt"], flagv_sb[0:1, fvcol:fvcol + 1])
        tgtv = make_scalar_value(rg["tgt"], min_val=0, max_val=(1 << 30))
        e.reg_mov(rg["fl"], 0)
        flv = make_scalar_value(rg["fl"], min_val=0, max_val=(1 << 30))
        cntv = make_scalar_value(rg["cnt"], min_val=0, max_val=(1 << 30))

        def one(row):
            with e.While(lambda: flv != tgtv):
                e.dma_start(out=pollbuf[0:1, slot:slot + 1],
                            in_=sh_flag[row:row + 1, col:col + 1]).then_inc(s_pol, 16)
                e.reg_add(rg["cnt"], rg["cnt"], 16)
                e.wait_ge(s_pol, cntv)
                e.reg_load(rg["fl"], pollbuf[0:1, slot:slot + 1])
        if row_if == row_else:
            one(row_if)
        else:
            with e.If(rg["role"] == 0):
                one(row_if)
            with e.Else():
                one(row_else)
            nc.end_ifs()

    for k in range(NCH):
        par = k % 2

        for sub in range(NSUB):
            ldn = add("sync", lambda e, k=k, sub=sub: e.dma_start(
                out=x_sub, in_=xb[k * TC + sub * 128:k * TC + (sub + 1) * 128, :]),
                waits=(("act", C["act"]), ("dve", DVE_CLR if k == 0 else C["dve"])),
                inc="ld")
            a1 = add("scalar", lambda e: e.activation(out=sqtmp, in_=x_sub,
                                                      func=AF.Square,
                                                      accum_out=stat_s[:, 1:2]),
                     waits=(("ld", ldn), ("dve", C["dve"])), inc="act")
            a2 = add("scalar", lambda e: e.activation(out=sqtmp, in_=x_sub,
                                                      func=AF.Copy,
                                                      accum_out=stat_s[:, 0:1]),
                     inc="act")

            def st1a(e):
                e.tensor_scalar_mul(out=stat_s[:, 2:3], in0=stat_s[:, 0:1],
                                    scalar1=1.0 / DM)
                return e.tensor_scalar_mul(out=stat_s[:, 3:4], in0=stat_s[:, 1:2],
                                           scalar1=1.0 / DM)
            w1 = add("vector", st1a, waits=(("act", a2),), inc="dve")
            w2 = add("vector", lambda e: e.tensor_tensor(
                out=stat_s[:, 4:5], in0=stat_s[:, 2:3], in1=stat_s[:, 2:3],
                op=ALU.mult), waits=(("dve", w1),), inc="dve")
            d1 = add("vector", lambda e: e.tensor_tensor(
                out=stat_s[:, 4:5], in0=stat_s[:, 3:4], in1=stat_s[:, 4:5],
                op=ALU.subtract), waits=(("dve", w2),), inc="dve")
            a3 = add("scalar", lambda e: e.activation(out=stat_s[:, 5:6],
                                                      in_=stat_s[:, 4:5],
                                                      func=AF.Sqrt, bias=eps_t),
                     waits=(("dve", d1),), inc="act")

            w3 = add("vector", lambda e: e.reciprocal(out=stat_s[:, 6:7],
                                                      in_=stat_s[:, 5:6]),
                     waits=(("act", a3),), inc="dve")
            w4 = add("vector", lambda e: e.tensor_tensor(
                out=stat_s[:, 7:8], in0=stat_s[:, 2:3], in1=stat_s[:, 6:7],
                op=ALU.mult), waits=(("dve", w3),), inc="dve")
            d2 = add("vector", lambda e: e.tensor_scalar_mul(
                out=stat_s[:, 7:8], in0=stat_s[:, 7:8], scalar1=-1.0),
                waits=(("dve", w4),), inc="dve")
            a4 = add("scalar", lambda e: e.activation(out=xln_tm, in_=x_sub,
                                                      func=AF.Identity,
                                                      scale=stat_s[:, 6:7],
                                                      bias=stat_s[:, 7:8]),
                     waits=(("dve", d2),), inc="act")
            if debug and k == 0 and sub == 0:
                add("sync", lambda e: e.dma_start(out=dbg["dbg_xtm"], in_=xln_tm),
                    waits=(("act", a4),), inc="st")
                add("sync", lambda e: e.dma_start(out=dbg["dbg_stat"], in_=stat_s),
                    waits=(("act", a4),), inc="st")
            for g in range(2):
                def tp(e, g=g):
                    tpb = ps_tp.bitcast(BF16)
                    last = None
                    for i in range(4):
                        blk = g * 4 + i
                        last = e.transpose(out=tpb[:, i * 128:(i + 1) * 128],
                                           in_=xln_tm[:, blk * 128:(blk + 1) * 128],
                                           identity=ident)
                    return last.then_inc(s_pe, 1)
                C["pe"] += 1
                pe_n = C["pe"]
                add("tensor", tp, waits=(("act", C["act"]),))
                a_ev = add("scalar", lambda e: e.activation(
                    out=cfin_b, in_=ps_tp.bitcast(BF16)[:, 0:512], func=AF.Copy),
                    waits=(("pe", pe_n), ("st", C["st"])), inc="act")
                add("sync", lambda e, g=g, sub=sub: e.dma_start(
                    out=bigscr.rearrange("p (c t) -> p c t", t=TC)[
                        :, 16 + g * 4:16 + (g + 1) * 4, sub * 128:(sub + 1) * 128],
                    in_=cfin_b.rearrange("p (c t) -> p c t", t=128)),
                    waits=(("act", a_ev),), inc="st")
        ACT_LN = C["act"]

        if k > 0:
            def halo(e):
                last = None
                for dbt in range(NDB):
                    base = dbt * (TC + 4)
                    last = e.tensor_copy(out=u_t[:, base + 1:base + 4],
                                         in_=u_t[:, base + TC + 1:base + TC + 4])
                return last.then_inc(s_dve, 1)
            C["dve"] += 1
            add("vector", halo, waits=(("act", ACT_LN),))
        DVE_HALO = C["dve"]

        for et in range(16):
            wb = et % 2
            ldn = add("sync", lambda e, et=et, wb=wb: e.dma_start(
                out=win_blk[:, wb * NDB * 128:(wb + 1) * NDB * 128].rearrange(
                    "p (kt e) -> p kt e", e=128),
                in_=win_b.rearrange("(kt p) e -> p kt e", p=128)[
                    :, :, et * 128:(et + 1) * 128]),
                waits=(("pe", max(0, C["pe"] - 1)),), inc="ld")
            for half in range(2):
                p = ps_a if half == 0 else ps_b

                def mmf(e, wb=wb, half=half, p=p):
                    last = None
                    for kt in range(NDB):
                        last = e.matmul(out=p,
                                        lhsT=win_blk[:, (wb * NDB + kt) * 128:
                                                     (wb * NDB + kt + 1) * 128],
                                        rhs=xlnf(kt)[:, half * 512:(half + 1) * 512],
                                        start=(kt == 0), stop=(kt == NDB - 1))
                    return last.then_inc(s_pe, 1)
                C["pe"] += 1
                pe_n = C["pe"]
                add("tensor", mmf, waits=(("ld", ldn), ("act", C["act"])))
                if et < 8:
                    a_uev = add("scalar", lambda e, et=et, half=half, p=p: e.activation(
                        out=u_t[:, et * (TC + 4) + 4 + half * 512:
                                et * (TC + 4) + 4 + (half + 1) * 512],
                        in_=p, func=AF.Copy),
                        waits=(("pe", pe_n), ("dve", DVE_HALO)), inc="act")
                    if half == 1:
                        dbt = et
                        base = dbt * (TC + 4)

                        def convf(e, dbt=dbt, base=base):
                            cv = sp_scr.bitcast(BF16)
                            tv = sg_scr.bitcast(BF16)
                            e.tensor_scalar(out=cv[:, 0:TC],
                                            in0=u_t[:, base + 1:base + 1 + TC],
                                            scalar1=convw_sb[:, dbt * 4:dbt * 4 + 1],
                                            scalar2=convb_sb[:, dbt:dbt + 1],
                                            op0=ALU.mult, op1=ALU.add)
                            last = None
                            for kk in range(1, 4):
                                e.tensor_scalar(
                                    out=tv[:, 0:TC],
                                    in0=u_t[:, base + 1 + kk:base + 1 + kk + TC],
                                    scalar1=convw_sb[:, dbt * 4 + kk:dbt * 4 + kk + 1],
                                    scalar2=None, op0=ALU.mult)
                                last = e.tensor_tensor(out=cv[:, 0:TC], in0=cv[:, 0:TC],
                                                       in1=tv[:, 0:TC], op=ALU.add)
                            return last.then_inc(s_dve, 1)
                        C["dve"] += 1
                        d_conv = C["dve"]
                        add("vector", convf, waits=(("act", C["act"]),))
                        if debug and k == 0 and dbt == 0:
                            add("sync", lambda e: e.dma_start(
                                out=dbg["dbg_cv"],
                                in_=sp_scr.bitcast(BF16)[:, 0:TC]),
                                waits=(("dve", d_conv),), inc="st")
                        add("scalar", lambda e, dbt=dbt: e.activation(
                            out=uc_t[:, dbt * TC:(dbt + 1) * TC],
                            in_=sp_scr.bitcast(BF16)[:, 0:TC], func=AF.Silu),
                            waits=(("dve", d_conv),), inc="act")
                else:
                    add("scalar", lambda e, et=et, half=half, p=p: e.activation(
                        out=zs_t[:, (et - 8) * TC + half * 512:
                                (et - 8) * TC + (half + 1) * 512],
                        in_=p, func=AF.Silu),
                        waits=(("pe", pe_n),), inc="act")
        ACT_XZ = C["act"]
        ACT_UC = C["act"]

        for half in range(2):
            p = ps_a if half == 0 else ps_b

            def mmf(e, half=half, p=p):
                last = None
                for kt in range(NDB):
                    last = e.matmul(out=p[0:96, :],
                                    lhsT=wxb_sb[:, kt * 96:(kt + 1) * 96],
                                    rhs=uc_t[:, kt * TC + half * 512:
                                             kt * TC + (half + 1) * 512],
                                    start=(kt == 0), stop=(kt == NDB - 1))
                return last.then_inc(s_pe, 1)
            C["pe"] += 1
            pe_n = C["pe"]
            add("tensor", mmf, waits=(("act", ACT_UC),))
            add("vector", lambda e, half=half, p=p: e.tensor_copy(
                out=xdbl_t[0:96, half * 512:(half + 1) * 512], in_=p[0:96, :]),
                waits=(("pe", pe_n),), inc="dve")
        DVE_XP = C["dve"]

        st_before = C["st"]

        def xch(e, k=k, st_before=st_before):
            rg = get_sync_regs(e)
            with e.If(rg["role"] == 0):
                e.dma_start(out=sh_xdbl[0, k, :, :], in_=xdbl_t[0:96, :]).then_inc(s_st, 16)
            with e.Else():
                e.dma_start(out=sh_xdbl[1, k, :, :], in_=xdbl_t[0:96, :]).then_inc(s_st, 16)
            nc.end_ifs()
            e.wait_ge(s_st, st_before + 16)
            with e.If(rg["role"] == 0):
                e.dma_start(out=sh_flag[0:1, k:k + 1],
                            in_=flagv_sb[0:1, k:k + 1]).then_inc(s_st, 16)
            with e.Else():
                e.dma_start(out=sh_flag[1:2, k:k + 1],
                            in_=flagv_sb[0:1, k:k + 1]).then_inc(s_st, 16)
            nc.end_ifs()
            emit_poll(e, 1, 0, k, k, 0)
            with e.If(rg["role"] == 0):
                e.dma_start(out=xdblp_t[0:96, :], in_=sh_xdbl[1, k, :, :]).then_inc(s_ld, 16)
            with e.Else():
                e.dma_start(out=xdblp_t[0:96, :], in_=sh_xdbl[0, k, :, :]).then_inc(s_ld, 16)
            nc.end_ifs()
        C["st"] += 32
        C["ld"] += 16
        add("sync", xch, waits=(("dve", DVE_XP),))
        LD_XD = C["ld"]

        def xsum(e):
            e.tensor_tensor(out=xdbl_t[0:96, :], in0=xdbl_t[0:96, :],
                            in1=xdblp_t[0:96, :], op=ALU.add)
            e.tensor_copy(out=dtr_b, in_=xdbl_t[0:64, :])
            return e.tensor_copy(out=bc_b, in_=xdbl_t[64:96, :])
        DVE_XS = add("vector", xsum, waits=(("ld", LD_XD),), inc="dve")

        if debug and k == 0:
            add("sync", lambda e: e.dma_start(out=dbg["dbg_xln"], in_=bigscr[:, 16 * TC:24 * TC]),
                waits=(("act", ACT_LN),), inc="st")
            add("sync", lambda e: e.dma_start(out=dbg["dbg_u"], in_=u_t),
                waits=(("act", ACT_XZ),), inc="st")
            add("sync", lambda e: e.dma_start(out=dbg["dbg_zs"], in_=zs_t),
                waits=(("act", ACT_XZ),), inc="st")
            add("sync", lambda e: e.dma_start(out=dbg["dbg_uc"], in_=uc_t),
                waits=(("act", ACT_UC),), inc="st")
            add("sync", lambda e: e.dma_start(out=dbg["dbg_xdbl"][0:96, :], in_=xdbl_t[0:96, :]),
                waits=(("dve", DVE_XS),), inc="st")
            add("sync", lambda e: e.dma_start(out=dbg["dbg_m"], in_=m_sb),
                waits=(("act", ACT_M),), inc="st")

        rep_dve_hist = {}
        for s in range(32):
            dst = brep(s) if s < 16 else crep(s - 16)
            pA, pB = (ps_a, ps_b) if s % 2 == 0 else (ps_q, ps_r)

            def mmf(e, s=s, pA=pA, pB=pB):
                e.matmul(out=pA, lhsT=selrep[:, s * 128:(s + 1) * 128],
                         rhs=bc_b[0:32, 0:512], start=True, stop=True)
                return e.matmul(out=pB, lhsT=selrep[:, s * 128:(s + 1) * 128],
                                rhs=bc_b[0:32, 512:1024], start=True, stop=True
                                ).then_inc(s_pe, 1)
            C["pe"] += 1
            pe_n = C["pe"]
            wm = [("dve", DVE_XS if s < 2 else rep_dve_hist[s - 2])]
            add("tensor", mmf, waits=tuple(wm))

            def evacf(e, dst=dst, pA=pA, pB=pB):
                e.tensor_copy(out=dst[:, 0:512], in_=pA)
                return e.tensor_copy(out=dst[:, 512:1024], in_=pB)
            rep_dve_hist[s] = add("vector", evacf, waits=(("pe", pe_n),), inc="dve")
        DVE_REP = C["dve"]

        for dbt in range(NDB):
            def dtmm(e, dbt=dbt):
                e.matmul(out=ps_q, lhsT=wdtb_sb[:, dbt * 128:(dbt + 1) * 128],
                         rhs=dtr_b[:, 0:512], start=True, stop=True)
                return e.matmul(out=ps_r, lhsT=wdtb_sb[:, dbt * 128:(dbt + 1) * 128],
                                rhs=dtr_b[:, 512:1024], start=True, stop=True
                                ).then_inc(s_pe, 1)
            C["pe"] += 1
            pe_dt = C["pe"]
            add("tensor", dtmm, waits=(("dve", C["dve"]),))

            def sp1(e, dbt=dbt):
                e.tensor_scalar(out=sg_scr[:, 0:512], in0=ps_q,
                                scalar1=bdt_sb[:, dbt:dbt + 1], scalar2=None,
                                op0=ALU.add)
                e.tensor_scalar(out=sg_scr[:, 512:1024], in0=ps_r,
                                scalar1=bdt_sb[:, dbt:dbt + 1], scalar2=None,
                                op0=ALU.add)
                return e.tensor_scalar_min(out=sp_scr, in0=sg_scr, scalar1=15.0)
            d_sp1 = add("vector", sp1, waits=(("pe", pe_dt),), inc="dve")
            a_sg = add("scalar", lambda e: e.activation(out=sp_scr, in_=sp_scr,
                                                        func=AF.Sigmoid, scale=-1.0),
                       waits=(("dve", d_sp1),), inc="act")
            a_ln = add("scalar", lambda e: e.activation(out=sp_scr, in_=sp_scr,
                                                        func=AF.Ln), inc="act")

            def sp2(e, dbt=dbt):
                e.scalar_tensor_tensor(out=dtb_t, in0=sp_scr,
                                       scalar=-1.0, in1=sg_scr,
                                       op0=ALU.mult, op1=ALU.max)
                return e.tensor_tensor(out=dtu_t, in0=dtb_t,
                                       in1=uc_t[:, dbt * TC:(dbt + 1) * TC], op=ALU.mult)
            DVE_DT = add("vector", sp2, waits=(("act", a_ln),), inc="dve")
            if debug and k == 0 and dbt == 0:
                add("sync", lambda e: e.dma_start(out=dbg["dbg_dt"], in_=dtb_t),
                    waits=(("dve", DVE_DT),), inc="st")
                add("sync", lambda e: e.dma_start(out=dbg["dbg_dtu"], in_=dtu_t),
                    waits=(("dve", DVE_DT),), inc="st")
                add("sync", lambda e: e.dma_start(out=dbg["dbg_brep"], in_=brep(0)),
                    waits=(("dve", DVE_DT),), inc="st")
                add("sync", lambda e: e.dma_start(out=dbg["dbg_crep"], in_=crep(0)),
                    waits=(("dve", DVE_DT),), inc="st")

            dve_hist = {}
            pe_hist = {}
            for s in range(S):
                pb = s % 2
                sl = slice(pb * TC, (pb + 1) * TC)
                wda = [("dve", DVE_DT)]
                if s >= 2:
                    wda.append(("dve", dve_hist[s - 2]))
                a_da = add("scalar", lambda e, dbt=dbt, s=s, sl=sl: e.activation(
                    out=dA_t[:, sl], in_=dtb_t, func=AF.Exp,
                    scale=aneg_sb[:, dbt * S + s:dbt * S + s + 1]),
                    waits=tuple(wda), inc="act")

                def sloop(e, dbt=dbt, s=s, sl=sl):
                    e.tensor_tensor(out=b_t, in0=dtu_t, in1=brep(s), op=ALU.mult)
                    idx = dbt * S + s
                    e.tensor_tensor_scan(out=h_t, data0=dA_t[:, sl], data1=b_t,
                                         initial=hstate[:, idx:idx + 1],
                                         op0=ALU.mult, op1=ALU.add)
                    e.tensor_tensor(out=mm_t[:, sl], in0=h_t,
                                    in1=crep(s), op=ALU.mult)
                    return e.tensor_copy(out=hstate[:, idx:idx + 1],
                                         in_=h_t[:, TC - 1:TC])
                wsl = [("act", a_da)]
                if s >= 2:
                    wsl.append(("pe", pe_hist[s - 2]))
                d_s = add("vector", sloop, waits=tuple(wsl), inc="dve")
                dve_hist[s] = d_s

                def ymm(e, s=s, lo=pb * TC):
                    e.matmul(out=ps_y0, lhsT=ident, rhs=mm_t[:, lo:lo + 512],
                             start=(s == 0), stop=(s == S - 1))
                    return e.matmul(out=ps_y1, lhsT=ident,
                                    rhs=mm_t[:, lo + 512:lo + 1024],
                                    start=(s == 0), stop=(s == S - 1)
                                    ).then_inc(s_pe, 1)
                C["pe"] += 1
                pe_hist[s] = C["pe"]
                add("tensor", ymm, waits=(("dve", d_s),))
            PE_YS = C["pe"]

            yb = dbt % 2

            def yfin(e, dbt=dbt, yb=yb):
                e.scalar_tensor_tensor(out=yfin_t[:, yb * TC:yb * TC + 512],
                                       in0=uc_t[:, dbt * TC:dbt * TC + 512],
                                       scalar=dvec_sb[:, dbt:dbt + 1], in1=ps_y0,
                                       op0=ALU.mult, op1=ALU.add)
                e.scalar_tensor_tensor(out=yfin_t[:, yb * TC + 512:(yb + 1) * TC],
                                       in0=uc_t[:, dbt * TC + 512:(dbt + 1) * TC],
                                       scalar=dvec_sb[:, dbt:dbt + 1], in1=ps_y1,
                                       op0=ALU.mult, op1=ALU.add)
                return e.tensor_tensor(out=yfin_t[:, yb * TC:(yb + 1) * TC],
                                       in0=yfin_t[:, yb * TC:(yb + 1) * TC],
                                       in1=zs_t[:, dbt * TC:(dbt + 1) * TC], op=ALU.mult)
            d_yf = add("vector", yfin,
                       waits=(("pe", PE_YS), ("st", C["st"] - 16 if dbt >= 2 else 0)),
                       inc="dve")

            def ystore(e, k=k, yb=yb, dbt=dbt):
                rg = get_sync_regs(e)
                with e.If(rg["role"] == 0):
                    e.dma_start(out=sh_y[0, k, dbt * 128:(dbt + 1) * 128, :],
                                in_=yfin_t[:, yb * TC:(yb + 1) * TC]).then_inc(s_st, 16)
                with e.Else():
                    e.dma_start(out=sh_y[1, k, dbt * 128:(dbt + 1) * 128, :],
                                in_=yfin_t[:, yb * TC:(yb + 1) * TC]).then_inc(s_st, 16)
                nc.end_ifs()
            C["st"] += 16
            add("sync", ystore, waits=(("dve", d_yf),))
            if debug and k == 0 and dbt == 0:
                add("sync", lambda e: e.dma_start(out=dbg["dbg_yfin"],
                                                  in_=yfin_t[:, 0:TC]),
                    waits=(("dve", d_yf),), inc="st")
                add("sync", lambda e: e.dma_start(out=dbg["dbg_hst"], in_=hstate),
                    waits=(("dve", d_yf),), inc="st")
        ST_Y = C["st"]

        def yflag(e, k=k, ST_Y=ST_Y):
            rg = get_sync_regs(e)
            e.wait_ge(s_st, ST_Y)
            with e.If(rg["role"] == 0):
                e.dma_start(out=sh_flag[2:3, k:k + 1],
                            in_=flagv_sb[0:1, NCH + k:NCH + k + 1]).then_inc(s_st, 16)
            with e.Else():
                e.dma_start(out=sh_flag[3:4, k:k + 1],
                            in_=flagv_sb[0:1, NCH + k:NCH + k + 1]).then_inc(s_st, 16)
            nc.end_ifs()
        C["st"] += 16
        add("sync", yflag)
        ST_YF = C["st"]

        ld0, st0, pe0, act0, dve0 = C["ld"], C["st"], C["pe"], C["act"], C["dve"]
        sc_parts, pc_parts, ac_parts, vc_parts = [], [], [], []

        for th in range(2):
            pe_attn_groups = []
            dve_banks_free = C["dve"]
            act_banks_free = C["act"]
            for kt in range(16):
                wb2 = kt % 2
                yb2 = kt % 2
                w_pe = pe_attn_groups[kt - 2] if kt >= 2 else 0
                C["ld"] += 16
                ld_w = C["ld"]

                def ldw(e, kt=kt, wb2=wb2, w_pe=w_pe):
                    if w_pe:
                        e.wait_ge(s_pe, w_pe)
                    e.dma_start(out=wout_blk[:, wb2 * DM:(wb2 + 1) * DM],
                                in_=wout_b[kt * 128:(kt + 1) * 128, :]).then_inc(s_ld, 16)
                sc_parts.append(ldw)
                C["ld"] += 16
                ld_y = C["ld"]

                def ldy(e, kt=kt, yb2=yb2, k=k, th=th):
                    e.dma_start(out=ya_t[:, yb2 * 512:(yb2 + 1) * 512],
                                in_=sh_y[kt // 8, k, (kt % 8) * 128:(kt % 8 + 1) * 128,
                                         th * 512:(th + 1) * 512]).then_inc(s_ld, 16)
                sc_parts.append(ldy)
                C["pe"] += 1
                pe_attn_groups.append(C["pe"])

                def attnmm(e, kt=kt, wb2=wb2, yb2=yb2, ld_y=ld_y,
                           dbf=dve_banks_free):
                    e.wait_ge(s_ld, ld_y)
                    if kt == 0:
                        e.wait_ge(s_dve, dbf)
                    last = None
                    for et in range(8):
                        last = e.matmul(out=banks[et],
                                        lhsT=wout_blk[:, wb2 * DM + et * 128:
                                                      wb2 * DM + (et + 1) * 128],
                                        rhs=ya_t[:, yb2 * 512:(yb2 + 1) * 512],
                                        start=(kt == 0), stop=(kt == 15))
                    last.then_inc(s_pe, 1)
                pc_parts.append(attnmm)
            PE_ATTN = C["pe"]

            for et in range(8):
                C["ld"] += 16
                ld_x = C["ld"]
                dve_prev = C["dve"]

                def ldx(e, et=et, k=k, th=th, w=dve_prev if et >= 1 else 0):
                    if w:
                        e.wait_ge(s_dve, w)
                    e.dma_start(out=xfm_t,
                                in_=xbt[et * 128:(et + 1) * 128,
                                        k * TC + th * 512:k * TC + (th + 1) * 512]
                                ).then_inc(s_ld, 16)
                sc_parts.append(ldx)
                C["dve"] += 1
                d_x1 = C["dve"]

                def x1add(e, et=et, ld_x=ld_x, PE_ATTN=PE_ATTN):
                    e.wait_ge(s_ld, ld_x)
                    e.wait_ge(s_pe, PE_ATTN)
                    e.tensor_tensor(out=x1f(et), in0=xfm_t,
                                    in1=banks[et], op=ALU.add).then_inc(s_dve, 1)
                vc_parts.append(x1add)
                C["act"] += 2

                def x1cast(e, et=et, d=d_x1):
                    e.wait_ge(s_dve, d)
                    e.activation(out=x1b(et), in_=x1f(et), func=AF.Copy).then_inc(s_act, 1)
                    e.activation(out=x1sq(et), in_=x1f(et),
                                 func=AF.Square).then_inc(s_act, 1)
                ac_parts.append(x1cast)
            ACT_X1 = C["act"]

            C["pe"] += 1
            pe_stats = C["pe"]

            def statsmm(e, ACT_X1=ACT_X1):
                e.wait_ge(s_act, ACT_X1)
                for et in range(8):
                    e.matmul(out=ps_st[0:1, :], lhsT=ones_col, rhs=x1b(et),
                             start=(et == 0), stop=(et == 7))
                last = None
                for et in range(8):
                    last = e.matmul(out=ps_q[0:1, :], lhsT=ones_col, rhs=x1sq(et),
                                    start=(et == 0), stop=(et == 7))
                last.then_inc(s_pe, 1)
            pc_parts.append(statsmm)

            C["dve"] += 1
            d_st1 = C["dve"]

            def statsv(e, pe_stats=pe_stats):
                e.wait_ge(s_pe, pe_stats)
                e.tensor_scalar_mul(out=musig[0:1, 0:512], in0=ps_st[0:1, :],
                                    scalar1=1.0 / DM)
                e.tensor_scalar_mul(out=musig[0:1, 512:1024], in0=ps_q[0:1, :],
                                    scalar1=1.0 / DM)
                e.tensor_tensor(out=musig[0:1, 1024:1536], in0=musig[0:1, 0:512],
                                in1=musig[0:1, 0:512], op=ALU.mult)
                e.tensor_tensor(out=musig[0:1, 1024:1536], in0=musig[0:1, 512:1024],
                                in1=musig[0:1, 1024:1536],
                                op=ALU.subtract).then_inc(s_dve, 1)
            vc_parts.append(statsv)
            C["act"] += 1
            a_sq = C["act"]

            def statsa(e, d=d_st1):
                e.wait_ge(s_dve, d)
                e.activation(out=musig[0:1, 512:1024], in_=musig[0:1, 1024:1536],
                             func=AF.Sqrt, bias=eps_t[0:1, :]).then_inc(s_act, 1)
            ac_parts.append(statsa)
            C["dve"] += 1
            d_st2 = C["dve"]

            def statsv2(e, a=a_sq):
                e.wait_ge(s_act, a)
                e.reciprocal(out=musig[0:1, 1024:1536], in_=musig[0:1, 512:1024])
                e.tensor_scalar_mul(out=musig[0:1, 512:1024], in0=musig[0:1, 1024:1536],
                                    scalar1=-1.0)
                e.tensor_copy(out=rowbf[0:1, 0:512], in_=musig[0:1, 512:1024])
                e.tensor_copy(out=rowbf[0:1, 512:1024],
                              in_=musig[0:1, 0:512]).then_inc(s_dve, 1)
            vc_parts.append(statsv2)
            C["pe"] += 1
            pe_rep = C["pe"]

            def repmm(e, d=d_st2):
                e.wait_ge(s_dve, d)
                e.matmul(out=ps_a, lhsT=ones_row, rhs=rowbf[0:1, 0:512],
                         start=True, stop=True)
                e.matmul(out=ps_b, lhsT=ones_row, rhs=rowbf[0:1, 512:1024],
                         start=True, stop=True).then_inc(s_pe, 1)
            pc_parts.append(repmm)
            C["dve"] += 1
            d_rep = C["dve"]

            def repev(e, p=pe_rep):
                e.wait_ge(s_pe, p)
                e.tensor_copy(out=s2rep, in_=ps_a)
                e.tensor_copy(out=mu2rep, in_=ps_b).then_inc(s_dve, 1)
            vc_parts.append(repev)

            d_fin_prev = {}
            for ot in range(8):
                pbank = banks[ot % 2]
                w_fin = d_fin_prev.get(ot - 2, d_rep)
                C["pe"] += 1
                pe_ffn = C["pe"]

                def ffnmm(e, ot=ot, pbank=pbank, w=w_fin):
                    e.wait_ge(s_dve, w)
                    last = None
                    for kt in range(8):
                        last = e.matmul(out=pbank,
                                        lhsT=m_sb[:, kt * DM + ot * 128:
                                                  kt * DM + (ot + 1) * 128],
                                        rhs=x1b(kt), start=(kt == 0), stop=(kt == 7))
                    last.then_inc(s_pe, 1)
                pc_parts.append(ffnmm)
                C["dve"] += 1
                d_fin = C["dve"]
                d_fin_prev[ot] = d_fin
                st_w = C["st"] if (ot > 0 or th > 0) else 0

                def ffnfin(e, ot=ot, pbank=pbank, p=pe_ffn, st_w=st_w):
                    e.wait_ge(s_pe, p)
                    if st_w:
                        e.wait_ge(s_st, st_w)
                    e.scalar_tensor_tensor(out=cfin_b, in0=mu2rep,
                                           scalar=mbar_sb[:, ot:ot + 1], in1=pbank,
                                           op0=ALU.mult, op1=ALU.subtract)
                    e.tensor_tensor(out=cfin_b, in0=cfin_b, in1=s2rep, op=ALU.mult)
                    e.scalar_tensor_tensor(out=outet_t,
                                           in0=cfin_b, scalar=fbias_sb[:, ot:ot + 1],
                                           in1=x1f(ot), op0=ALU.add,
                                           op1=ALU.add).then_inc(s_dve, 1)
                vc_parts.append(ffnfin)
                C["st"] += 16

                def ostore(e, ot=ot, k=k, th=th, d=d_fin):
                    e.wait_ge(s_dve, d)
                    e.dma_start(out=out_d[k // 2, ot * 128:(ot + 1) * 128,
                                          th * 512:(th + 1) * 512],
                                in_=outet_t).then_inc(s_st, 16)
                sc_parts.append(ostore)

        n_ld_c = C["ld"] - ld0
        n_st_c = C["st"] - st0
        n_pe_c = C["pe"] - pe0
        n_act_c = C["act"] - act0
        n_dve_c = C["dve"] - dve0

        def sync_c(e, k=k, par=par, parts=tuple(sc_parts), n_ld=n_ld_c, n_st=n_st_c,
                   ST_YF=ST_YF):
            rg = get_sync_regs(e)
            cond = (rg["role"] == 0) if par == 0 else (rg["role"] == 1)
            with e.If(cond):
                row = 3 if par == 0 else 2
                e.reg_load(rg["tgt"], flagv_sb[0:1, NCH + k:NCH + k + 1])
                tgtv = make_scalar_value(rg["tgt"], min_val=0, max_val=(1 << 30))
                e.reg_mov(rg["fl"], 0)
                flv = make_scalar_value(rg["fl"], min_val=0, max_val=(1 << 30))
                cntv = make_scalar_value(rg["cnt"], min_val=0, max_val=(1 << 30))
                with e.While(lambda: flv != tgtv):
                    e.dma_start(out=pollbuf[0:1, 1:2],
                                in_=sh_flag[row:row + 1, k:k + 1]).then_inc(s_pol, 16)
                    e.reg_add(rg["cnt"], rg["cnt"], 16)
                    e.wait_ge(s_pol, cntv)
                    e.reg_load(rg["fl"], pollbuf[0:1, 1:2])
                e.wait_ge(s_st, ST_YF)
                for p in parts:
                    p(e)
            with e.Else():
                e.sem_inc(s_ld, n_ld)
                e.sem_inc(s_st, n_st)
            nc.end_ifs()
        add("sync", sync_c)

        def mk_wrap(parts, sem_, n, par=par, k=k):
            parts = tuple(parts)
            name = f"rw{k}_{id(parts) % 100000}"

            def f(e, name=name):
                r = e.alloc_register(name)
                e.wait_ge(s_ld, LD_PRO)
                e.reg_load(r, role_sb[0:1, 0:1])
                rv = make_scalar_value(r, min_val=0, max_val=1)
                cond = (rv == 0) if par == 0 else (rv == 1)
                with e.If(cond):
                    for p in parts:
                        p(e)
                with e.Else():
                    if n > 0:
                        e.sem_inc(sem_, n)
                nc.end_ifs()
                e.free_register(r)
            return f

        add("tensor", mk_wrap(pc_parts, s_pe, n_pe_c))
        add("scalar", mk_wrap(ac_parts, s_act, n_act_c))
        add("vector", mk_wrap(vc_parts, s_dve, n_dve_c))

    FINAL_ST = C["st"]
    add("sync", lambda e: e.wait_ge(s_st, FINAL_ST))

    with nc.Block() as block:
        @block.sync
        def _(e):
            for f in steps["sync"]:
                f(e)

        @block.tensor
        def _(e):
            for f in steps["tensor"]:
                f(e)

        @block.scalar
        def _(e):
            for f in steps["scalar"]:
                f(e)

        @block.vector
        def _(e):
            for f in steps["vector"]:
                f(e)

    for g in reversed(ctx):
        g.__exit__(None, None, None)
    return nc



def _selrep():
    m = np.zeros((32, 32 * 128), dtype=np.float32)
    for s in range(32):
        m[s, s * 128:(s + 1) * 128] = 1.0
    return m.astype(bf16)


def _dft_tables():
    n = np.arange(DM, dtype=np.float64)
    ang = 2.0 * np.pi / DM * np.outer(n, n)
    ct = np.cos(ang).astype(np.float32).astype(bf16)
    st = np.sin(ang).astype(np.float32).astype(bf16)
    return ct, st


def _per_core_inputs(inputs, c, T, nonce):
    b, j = c // 2, c % 2
    NCH = T // TC
    f32 = np.float32
    W_in = np.asarray(inputs["L0_attn_W_in"], f32)
    W_x = np.asarray(inputs["L0_attn_W_x"], f32)
    W_dt = np.asarray(inputs["L0_attn_W_dt"], f32)
    W_out = np.asarray(inputs["L0_attn_W_out"], f32)
    x = np.ascontiguousarray(np.asarray(inputs["L0_attn_x"], f32)[b, :T])
    lo, hi = j * DH, (j + 1) * DH

    def tile128(v):
        return np.ascontiguousarray(v.reshape(NDB, 128).T)

    ct, st = _dft_tables()
    sel = np.concatenate([W_in[lo:hi], W_in[DI + lo:DI + hi]], axis=0)
    alog = np.asarray(inputs["L0_attn_A_log"], f32)[lo:hi]
    alog_t = np.ascontiguousarray(
        alog.reshape(NDB, 128, S).transpose(1, 0, 2).reshape(128, NDB * S))
    convw = np.asarray(inputs["L0_attn_conv_w"], f32)[lo:hi]
    convw_t = np.ascontiguousarray(
        convw.reshape(NDB, 128, 4).transpose(1, 0, 2).reshape(128, NDB * 4))
    flagv = (np.arange(1, 2 * NCH + 1, dtype=np.uint64) + np.uint64(nonce)) % np.uint64(2**32)
    return {
        "xb": x,
        "xbt": np.ascontiguousarray(x.T),
        "win_t": np.ascontiguousarray(sel.T),
        "wx_t": np.ascontiguousarray(W_x[:, lo:hi].T),
        "wdt_t": np.ascontiguousarray(W_dt[lo:hi].T),
        "bdt_in": tile128(np.asarray(inputs["L0_attn_b_dt"], f32)[lo:hi]),
        "alog_in": alog_t,
        "convw_in": convw_t,
        "convb_in": tile128(np.asarray(inputs["L0_attn_conv_b"], f32)[lo:hi]),
        "dvec_in": tile128(np.asarray(inputs["L0_attn_D"], f32)[lo:hi]),
        "wout_t": np.ascontiguousarray(W_out.T),
        "wr_t": np.ascontiguousarray(np.asarray(inputs["L0_ffn_w_freq_real"], f32).T),
        "wi_t": np.ascontiguousarray(np.asarray(inputs["L0_ffn_w_freq_imag"], f32).T),
        "ct_in": ct,
        "st_in": st,
        "fbias_in": tile128(np.asarray(inputs["L0_ffn_bias"], f32)),
        "ident_in": np.eye(128, dtype=np.float32).astype(bf16),
        "selrep_in": _selrep(),
        "role_in": np.array([[j, 0, 0, 0]], dtype=np.uint32),
        "flagv_in": flagv.astype(np.uint32).reshape(1, -1),
    }


_CACHE = {}


def kernel(**inputs):
    T = int(np.asarray(inputs["L0_attn_x"]).shape[1])
    if T not in _CACHE:
        _CACHE[T] = build_program(T)
    nc = _CACHE[T]
    nonce = int.from_bytes(np.random.bytes(4), "little")
    in_maps = [_per_core_inputs(inputs, c, T, nonce) for c in range(8)]
    res = run_bass_kernel_spmd(nc, in_maps, core_ids=list(range(8)))
    NCH = T // TC
    out = np.empty((B, T, DM), dtype=np.float32)
    for c in range(8):
        b, j = c // 2, c % 2
        o = res.results[c]["out_d"]
        for i in range(NCH // 2):
            k = 2 * i + j
            out[b, k * TC:(k + 1) * TC, :] = o[i].T
    return out


# version 05
# speedup vs baseline: 1.8680x; 1.4369x over previous
import numpy as np
import ml_dtypes
import concourse.bass as bass
import concourse.mybir as mybir
from concourse.bass_utils import run_bass_kernel_spmd
from concourse.expressions import make_scalar_value

F32 = mybir.dt.float32
BF16 = mybir.dt.bfloat16
U32 = mybir.dt.uint32
AF = mybir.ActivationFunctionType
ALU = mybir.AluOpType
bf16 = ml_dtypes.bfloat16

B = 4
DM = 1024
DI = 2048
DH = 1024
S = 16
RK = 64
NDB = DH // 128
TC = 1024


def build_program(T, debug=False):
    NCH = T // TC
    NSUB = TC // 128
    nc = bass.Bass("TRN2", target_bir_lowering=False, detect_race_conditions=False)

    def din(name, shape, dt=F32):
        return nc.dram_tensor(name, shape, dt, kind="ExternalInput").ap()

    xb = din("xb", [T, DM])
    xbt = din("xbt", [DM, T])
    win_t = din("win_t", [DM, 2 * DH])
    wx_t = din("wx_t", [DH, 96])
    wdt_t = din("wdt_t", [RK, DH])
    bdt_in = din("bdt_in", [128, NDB])
    alog_in = din("alog_in", [128, NDB * S])
    convw_in = din("convw_in", [128, NDB * 4])
    convb_in = din("convb_in", [128, NDB])
    dvec_in = din("dvec_in", [128, NDB])
    wout_t = din("wout_t", [DI, DM])
    wr_t = din("wr_t", [DM, DM])
    wi_t = din("wi_t", [DM, DM])
    ct_in = din("ct_in", [DM, DM], BF16)
    st_in = din("st_in", [DM, DM], BF16)
    fbias_in = din("fbias_in", [128, NDB])
    ident_in = din("ident_in", [128, 128], BF16)
    selrep_in = din("selrep_in", [32, 32 * 128], BF16)
    role_in = din("role_in", [1, 4], U32)
    flagv_in = din("flagv_in", [1, 2 * NCH], U32)

    out_d = nc.dram_tensor("out_d", [NCH // 2, DM, TC], F32, kind="ExternalOutput").ap()

    dbg = {}
    if debug:
        for nm, shape, dt_ in [("dbg_xln", [128, 8 * TC], BF16),
                               ("dbg_uc", [128, NDB * TC], BF16),
                               ("dbg_zs", [128, NDB * TC], BF16),
                               ("dbg_u", [128, NDB * (TC + 4)], BF16),
                               ("dbg_xdbl", [128, TC], F32),
                               ("dbg_dt", [128, TC], BF16),
                               ("dbg_dtu", [128, TC], BF16),
                               ("dbg_brep", [128, TC], BF16),
                               ("dbg_crep", [128, TC], BF16),
                               ("dbg_yfin", [128, TC], BF16),
                               ("dbg_m", [128, NDB * DM], BF16),
                               ("dbg_hst", [128, 128], F32),
                               ("dbg_cv", [128, TC], BF16),
                               ("dbg_xtm", [128, TC], BF16),
                               ("dbg_stat", [128, 8], F32)]:
            dbg[nm] = nc.dram_tensor(nm, shape, dt_, kind="ExternalOutput").ap()
    win_b = nc.dram_tensor("win_b", [DM, 2 * DH], BF16, kind="Internal").ap()
    wout_b = nc.dram_tensor("wout_b", [DI, DM], BF16, kind="Internal").ap()

    sh_xdbl = nc.dram_tensor("sh_xdbl", [2, NCH, 96, TC], F32, kind="Internal",
                             addr_space="Shared").ap()
    sh_y = nc.dram_tensor("sh_y", [2, NCH, DH, TC], BF16, kind="Internal",
                          addr_space="Shared").ap()
    sh_flag = nc.dram_tensor("sh_flag", [4, NCH], U32, kind="Internal",
                             addr_space="Shared").ap()

    ctx = []

    def sbuf(name, shape, dtype):
        g = nc.sbuf_tensor(name, shape, dtype)
        t = g.__enter__()
        ctx.append(g)
        return t[:, :] if hasattr(t, "__getitem__") else t

    def psum(name, shape):
        g = nc.psum_tensor(name, shape, F32)
        t = g.__enter__()
        ctx.append(g)
        return t[:, :]

    def mksem(name, num=None):
        g = nc.semaphore(name, num=num)
        t = g.__enter__()
        ctx.append(g)
        return t

    bigscr = sbuf("bigscr", [128, 32 * TC], BF16)
    u_t = sbuf("u_t", [128, NDB * (TC + 4)], BF16)
    zs_t = sbuf("zs_t", [128, NDB * TC], BF16)
    uc_t = sbuf("uc_t", [128, NDB * TC], BF16)
    m_sb = sbuf("m_sb", [128, NDB * DM], BF16)
    dA_t = sbuf("dA_t", [128, 2 * TC], BF16)
    b_t = sbuf("b_t", [128, TC], BF16)
    h_t = sbuf("h_t", [128, TC], BF16)
    mm_t = sbuf("mm_t", [128, 2 * TC], BF16)
    yfin_t = sbuf("yfin_t", [128, 2 * TC], BF16)
    dtb_t = sbuf("dtb_t", [128, TC], BF16)
    dtu_t = sbuf("dtu_t", [128, TC], BF16)
    sg_scr = sbuf("sg_scr", [128, TC], F32)
    sp_scr = sbuf("sp_scr", [128, TC], F32)
    cfin_b = sbuf("cfin_b", [128, 512], BF16)
    x_sub = sbuf("x_sub", [128, DM], F32)
    xdbl_t = sbuf("xdbl_t", [128, TC], F32)
    dtr_b = sbuf("dtr_b", [64, TC], BF16)
    bc_b = sbuf("bc_b", [32, TC], BF16)
    win_blk = sbuf("win_blk", [128, 2 * NDB * 128], BF16)
    wout_blk = sbuf("wout_blk", [128, 2 * DM], BF16)
    ya_t = sbuf("ya_t", [128, 2 * 512], BF16)
    xfm_t = sbuf("xfm_t", [128, 512], F32)
    outet_t = sbuf("outet_t", [128, 512], F32)
    ident = sbuf("ident", [128, 128], BF16)
    ones_col = sbuf("ones_col", [128, 1], BF16)
    ones_row = sbuf("ones_row", [1, 128], BF16)
    bdt_sb = sbuf("bdt_sb", [128, NDB], F32)
    aneg_sb = sbuf("aneg_sb", [128, NDB * S], F32)
    convw_sb = sbuf("convw_sb", [128, NDB * 4], F32)
    convb_sb = sbuf("convb_sb", [128, NDB], F32)
    dvec_sb = sbuf("dvec_sb", [128, NDB], F32)
    fbias_sb = sbuf("fbias_sb", [128, NDB], F32)
    role_sb = sbuf("role_sb", [1, 4], U32)
    flagv_sb = sbuf("flagv_sb", [1, 2 * NCH], U32)
    pollbuf = sbuf("pollbuf", [1, 4], U32)
    hstate = sbuf("hstate", [128, 128], F32)
    wxb_sb = sbuf("wxb_sb", [128, NDB * 96], BF16)
    selrep = sbuf("selrep", [32, 32 * 128], BF16)
    wdtb_sb = sbuf("wdtb_sb", [64, DH], BF16)
    mbar_sb = sbuf("mbar_sb", [128, NDB], F32)
    stat_s = sbuf("stat_s", [128, 8], F32)
    eps_t = sbuf("eps_t", [128, 1], F32)
    musig = sbuf("musig", [1, 3 * 512], F32)
    rowbf = sbuf("rowbf", [1, 2 * 512], BF16)

    xdblp_t = None

    def brep(s):
        return bigscr[:, s * TC:(s + 1) * TC]

    def crep(s):
        return bigscr[:, (16 + s) * TC:(17 + s) * TC]

    def xlnf(kt):
        return bigscr[:, (16 + kt) * TC:(17 + kt) * TC]

    xln_tm = bigscr[:, 0:TC]
    xdblp_t = bigscr[:, 0:2 * TC].bitcast(F32)
    sqtmp = bigscr[:, TC:2 * TC]
    castbuf = zs_t[:, 0:4 * TC].bitcast(F32)
    castout = u_t[:, 0:2 * TC]

    def wrb(kt):
        return uc_t[:, kt * DM:(kt + 1) * DM]

    def wib(kt):
        return bigscr[:, (16 + kt) * DM:(17 + kt) * DM]

    def ctb(kt):
        return zs_t[:, kt * DM:(kt + 1) * DM]

    def stbl(kt):
        return u_t[:, kt * (TC + 4):kt * (TC + 4) + DM]

    def Ureg(kt):
        return bigscr[:, kt * DM:(kt + 1) * DM]

    def Vreg(kt):
        return bigscr[:, (24 + kt) * DM:(25 + kt) * DM]

    big32 = bigscr.bitcast(F32)

    def x1f(et):
        return big32[:, et * 512:(et + 1) * 512]

    def x1b(et):
        return bigscr[:, 8 * TC + et * 512:8 * TC + (et + 1) * 512]

    def x1sq(et):
        return bigscr[:, 12 * TC + et * 512:12 * TC + (et + 1) * 512]

    s2rep = bigscr[:, 16 * TC:16 * TC + 512]
    mu2rep = bigscr[:, 16 * TC + 512:17 * TC]

    banks = [psum(f"psb{i}", [128, 512]) for i in range(8)]
    ps_tp = banks[0]
    ps_a = banks[1]
    ps_b = banks[2]
    ps_y0 = banks[3]
    ps_y1 = banks[4]
    ps_st = banks[5]
    ps_q = banks[6]
    ps_r = banks[7]

    s_ld = mksem("s_ld")
    s_st = mksem("s_st")
    s_pe = mksem("s_pe")
    s_act = mksem("s_act")
    s_dve = mksem("s_dve")
    s_pol = mksem("s_pol", num=200)
    SEMS = {"ld": s_ld, "st": s_st, "pe": s_pe, "act": s_act, "dve": s_dve}
    C = {"ld": 0, "st": 0, "pe": 0, "act": 0, "dve": 0}
    steps = {"sync": [], "tensor": [], "scalar": [], "vector": []}

    def add(eng, body, waits=(), inc=None):
        wl = tuple((SEMS[n], v) for (n, v) in waits if v > 0)
        if inc is None:
            def run(e, wl=wl, body=body):
                for sem_, v in wl:
                    e.wait_ge(sem_, v)
                body(e)
            steps[eng].append(run)
            return None
        amt = 16 if inc in ("ld", "st") else 1
        C[inc] += amt
        sem_i = SEMS[inc]

        def run(e, wl=wl, body=body, sem_i=sem_i, amt=amt):
            for sem_, v in wl:
                e.wait_ge(sem_, v)
            inst = body(e)
            inst.then_inc(sem_i, amt)
        steps[eng].append(run)
        return C[inc]

    for dst, src in [(ident, ident_in), (selrep, selrep_in), (bdt_sb, bdt_in), (aneg_sb, alog_in),
                     (convw_sb, convw_in), (convb_sb, convb_in), (dvec_sb, dvec_in),
                     (fbias_sb, fbias_in), (role_sb, role_in), (flagv_sb, flagv_in)]:
        add("sync", lambda e, d=dst, s=src: e.dma_start(out=d, in_=s), inc="ld")
    LD_PRO = C["ld"]

    add("scalar", lambda e: e.activation(out=aneg_sb, in_=aneg_sb, func=AF.Exp),
        waits=(("ld", LD_PRO),), inc="act")
    ACT_PRO = C["act"]

    def dve_pro(e):
        e.tensor_scalar_mul(out=aneg_sb, in0=aneg_sb, scalar1=-1.0)
        e.memset(ones_col, 1.0)
        e.memset(ones_row, 1.0)
        e.memset(hstate, 0.0)
        e.memset(eps_t, 1e-5)
        return e.memset(pollbuf, 0)
    add("vector", dve_pro, waits=(("act", ACT_PRO), ("ld", LD_PRO)), inc="dve")

    cast_rounds = ([("win", i) for i in range(8)] + [("wout", i) for i in range(16)]
                   + [("wx", i) for i in range(NDB)] + [("wr", i) for i in range(NDB)]
                   + [("wi", i) for i in range(NDB)] + [("wdt", 0)])
    for kind, i in cast_rounds:
        if kind == "win":
            src, cols = win_t[i * 128:(i + 1) * 128, :], 2 * DH
        elif kind == "wout":
            src, cols = wout_t[i * 128:(i + 1) * 128, :], DM
        elif kind == "wx":
            src, cols = wx_t[i * 128:(i + 1) * 128, :], 96
        elif kind == "wr":
            src, cols = wr_t[i * 128:(i + 1) * 128, :], DM
        elif kind == "wdt":
            src, cols = wdt_t, DH
        else:
            src, cols = wi_t[i * 128:(i + 1) * 128, :], DM
        dst_cb = castbuf[0:64, 0:DH] if kind == "wdt" else castbuf[:, 0:cols]
        ldn = add("sync", lambda e, s=src, d=dst_cb: e.dma_start(out=d, in_=s),
                  waits=(("dve", C["dve"]), ("st", C["st"])), inc="ld")
        if kind == "win":
            dv = add("vector", lambda e: e.tensor_copy(out=castout[:, 0:2 * DH],
                                                       in_=castbuf[:, 0:2 * DH]),
                     waits=(("ld", ldn),), inc="dve")
            add("sync", lambda e, i=i: e.dma_start(out=win_b[i * 128:(i + 1) * 128, :],
                                                   in_=castout[:, 0:2 * DH]),
                waits=(("dve", dv),), inc="st")
        elif kind == "wout":
            dv = add("vector", lambda e: e.tensor_copy(out=castout[:, 0:DM],
                                                       in_=castbuf[:, 0:DM]),
                     waits=(("ld", ldn),), inc="dve")
            add("sync", lambda e, i=i: e.dma_start(out=wout_b[i * 128:(i + 1) * 128, :],
                                                   in_=castout[:, 0:DM]),
                waits=(("dve", dv),), inc="st")
        elif kind == "wx":
            add("vector", lambda e, i=i: e.tensor_copy(out=wxb_sb[:, i * 96:(i + 1) * 96],
                                                       in_=castbuf[:, 0:96]),
                waits=(("ld", ldn),), inc="dve")
        elif kind == "wdt":
            add("vector", lambda e: e.tensor_copy(out=wdtb_sb,
                                                  in_=castbuf[0:64, 0:DH]),
                waits=(("ld", ldn),), inc="dve")
        elif kind == "wr":
            add("vector", lambda e, i=i: e.tensor_copy(out=wrb(i), in_=castbuf[:, 0:DM]),
                waits=(("ld", ldn),), inc="dve")
        else:
            add("vector", lambda e, i=i: e.tensor_copy(out=wib(i), in_=castbuf[:, 0:DM]),
                waits=(("ld", ldn),), inc="dve")
    DVE_CAST = C["dve"]

    for kt in range(NDB):
        add("sync", lambda e, kt=kt: e.dma_start(out=ctb(kt),
                                                 in_=ct_in[kt * 128:(kt + 1) * 128, :]),
            waits=(("dve", DVE_CAST),), inc="ld")
        add("sync", lambda e, kt=kt: e.dma_start(out=stbl(kt),
                                                 in_=st_in[kt * 128:(kt + 1) * 128, :]),
            inc="ld")
    LD_TBL = C["ld"]

    for which in range(2):
        for rt in range(NDB):
            for half in range(2):
                cs = slice(half * 512, (half + 1) * 512)
                if which == 0:
                    def mmf(e, rt=rt, cs=cs):
                        for kt in range(NDB):
                            e.matmul(out=ps_a, lhsT=wrb(kt)[:, rt * 128:(rt + 1) * 128],
                                     rhs=ctb(kt)[:, cs], start=(kt == 0),
                                     stop=(kt == NDB - 1))
                        last = None
                        for kt in range(NDB):
                            last = e.matmul(out=ps_b,
                                            lhsT=wib(kt)[:, rt * 128:(rt + 1) * 128],
                                            rhs=stbl(kt)[:, cs], start=(kt == 0),
                                            stop=(kt == NDB - 1))
                        return last.then_inc(s_pe, 1)
                else:
                    def mmf(e, rt=rt, cs=cs):
                        n = 0
                        last = None
                        for lf, rf in [(wib, ctb), (wrb, stbl)]:
                            for kt in range(NDB):
                                n += 1
                                last = e.matmul(out=ps_a,
                                                lhsT=lf(kt)[:, rt * 128:(rt + 1) * 128],
                                                rhs=rf(kt)[:, cs], start=(n == 1),
                                                stop=(n == 16))
                        return last.then_inc(s_pe, 1)
                C["pe"] += 1
                pe_n = C["pe"]
                add("tensor", mmf, waits=(("ld", LD_TBL), ("dve", C["dve"])))
                if which == 0:
                    def uevac(e, rt=rt, half=half):
                        e.tensor_copy(out=cfin_b, in_=ps_b)
                        return e.tensor_tensor(
                            out=Ureg(rt)[:, half * 512:(half + 1) * 512],
                            in0=ps_a, in1=cfin_b, op=ALU.subtract)
                    add("vector", uevac, waits=(("pe", pe_n),), inc="dve")
                else:
                    add("vector", lambda e, rt=rt, half=half: e.tensor_copy(
                        out=Vreg(rt)[:, half * 512:(half + 1) * 512], in_=ps_a),
                        waits=(("pe", pe_n),), inc="dve")
    DVE_UV = C["dve"]

    for nt in range(NDB):
        for half in range(2):
            cs = slice(half * 512, (half + 1) * 512)

            def mmf(e, nt=nt, cs=cs):
                n = 0
                last = None
                for lf, rf in [(ctb, Ureg), (stbl, Vreg)]:
                    for rt in range(NDB):
                        n += 1
                        last = e.matmul(out=ps_a, lhsT=lf(rt)[:, nt * 128:(nt + 1) * 128],
                                        rhs=rf(rt)[:, cs], start=(n == 1), stop=(n == 16))
                return last.then_inc(s_pe, 1)
            C["pe"] += 1
            pe_n = C["pe"]
            add("tensor", mmf, waits=(("dve", DVE_UV), ("act", C["act"])))
            add("scalar", lambda e, nt=nt, half=half: e.activation(
                out=m_sb[:, nt * DM + half * 512:nt * DM + (half + 1) * 512],
                in_=ps_a, func=AF.Copy, scale=1.0 / DM),
                waits=(("pe", pe_n),), inc="act")
    ACT_M = C["act"]

    for ot in range(NDB):
        def mmf(e, ot=ot):
            last = None
            for nt in range(NDB):
                last = e.matmul(out=ps_st[:, 0:1],
                                lhsT=m_sb[:, nt * DM + ot * 128:nt * DM + (ot + 1) * 128],
                                rhs=ones_col, start=(nt == 0), stop=(nt == NDB - 1))
            return last.then_inc(s_pe, 1)
        C["pe"] += 1
        pe_n = C["pe"]
        add("tensor", mmf, waits=(("act", ACT_M), ("dve", C["dve"])))
        add("vector", lambda e, ot=ot: e.tensor_copy(out=mbar_sb[:, ot:ot + 1],
                                                     in_=ps_st[:, 0:1]),
            waits=(("pe", pe_n),), inc="dve")
    PE_M = C["pe"]

    def clr(e):
        last = None
        for dbt in range(NDB):
            last = e.memset(u_t[:, dbt * (TC + 4):dbt * (TC + 4) + 4], 0.0)
        return last.then_inc(s_dve, 1)
    C["dve"] += 1
    add("vector", clr, waits=(("pe", PE_M),))
    DVE_CLR = C["dve"]

    sync_regs = {}

    def get_sync_regs(e):
        if "role" not in sync_regs:
            e.wait_ge(s_ld, LD_PRO)
            r = e.alloc_register("r_role")
            e.reg_load(r, role_sb[0:1, 0:1])
            sync_regs["role"] = make_scalar_value(r, min_val=0, max_val=1)
            sync_regs["fl"] = e.alloc_register("r_fl")
            sync_regs["cnt"] = e.alloc_register("r_cnt")
            e.reg_mov(sync_regs["cnt"], 0)
            sync_regs["tgt"] = e.alloc_register("r_tgt")
        return sync_regs

    def emit_poll(e, row_if, row_else, col, fvcol, slot):
        rg = get_sync_regs(e)
        e.reg_load(rg["tgt"], flagv_sb[0:1, fvcol:fvcol + 1])
        tgtv = make_scalar_value(rg["tgt"], min_val=0, max_val=(1 << 30))
        e.reg_mov(rg["fl"], 0)
        flv = make_scalar_value(rg["fl"], min_val=0, max_val=(1 << 30))
        cntv = make_scalar_value(rg["cnt"], min_val=0, max_val=(1 << 30))

        def one(row):
            with e.While(lambda: flv != tgtv):
                e.dma_start(out=pollbuf[0:1, slot:slot + 1],
                            in_=sh_flag[row:row + 1, col:col + 1]).then_inc(s_pol, 16)
                e.reg_add(rg["cnt"], rg["cnt"], 16)
                e.wait_ge(s_pol, cntv)
                e.reg_load(rg["fl"], pollbuf[0:1, slot:slot + 1])
        if row_if == row_else:
            one(row_if)
        else:
            with e.If(rg["role"] == 0):
                one(row_if)
            with e.Else():
                one(row_else)
            nc.end_ifs()

    for k in range(NCH):
        par = k % 2

        for sub in range(NSUB):
            ldn = add("sync", lambda e, k=k, sub=sub: e.dma_start(
                out=x_sub, in_=xb[k * TC + sub * 128:k * TC + (sub + 1) * 128, :]),
                waits=(("act", C["act"]), ("dve", DVE_CLR if k == 0 else C["dve"])),
                inc="ld")
            a1 = add("scalar", lambda e: e.activation(out=sqtmp, in_=x_sub,
                                                      func=AF.Square,
                                                      accum_out=stat_s[:, 1:2]),
                     waits=(("ld", ldn), ("dve", C["dve"])), inc="act")
            a2 = add("scalar", lambda e: e.activation(out=sqtmp, in_=x_sub,
                                                      func=AF.Copy,
                                                      accum_out=stat_s[:, 0:1]),
                     inc="act")

            def st1a(e):
                e.tensor_scalar_mul(out=stat_s[:, 2:3], in0=stat_s[:, 0:1],
                                    scalar1=1.0 / DM)
                return e.tensor_scalar_mul(out=stat_s[:, 3:4], in0=stat_s[:, 1:2],
                                           scalar1=1.0 / DM)
            w1 = add("vector", st1a, waits=(("act", a2),), inc="dve")
            w2 = add("vector", lambda e: e.tensor_tensor(
                out=stat_s[:, 4:5], in0=stat_s[:, 2:3], in1=stat_s[:, 2:3],
                op=ALU.mult), waits=(("dve", w1),), inc="dve")
            d1 = add("vector", lambda e: e.tensor_tensor(
                out=stat_s[:, 4:5], in0=stat_s[:, 3:4], in1=stat_s[:, 4:5],
                op=ALU.subtract), waits=(("dve", w2),), inc="dve")
            a3 = add("scalar", lambda e: e.activation(out=stat_s[:, 5:6],
                                                      in_=stat_s[:, 4:5],
                                                      func=AF.Sqrt, bias=eps_t),
                     waits=(("dve", d1),), inc="act")

            w3 = add("vector", lambda e: e.reciprocal(out=stat_s[:, 6:7],
                                                      in_=stat_s[:, 5:6]),
                     waits=(("act", a3),), inc="dve")
            w4 = add("vector", lambda e: e.tensor_tensor(
                out=stat_s[:, 7:8], in0=stat_s[:, 2:3], in1=stat_s[:, 6:7],
                op=ALU.mult), waits=(("dve", w3),), inc="dve")
            d2 = add("vector", lambda e: e.tensor_scalar_mul(
                out=stat_s[:, 7:8], in0=stat_s[:, 7:8], scalar1=-1.0),
                waits=(("dve", w4),), inc="dve")
            a4 = add("scalar", lambda e: e.activation(out=xln_tm, in_=x_sub,
                                                      func=AF.Identity,
                                                      scale=stat_s[:, 6:7],
                                                      bias=stat_s[:, 7:8]),
                     waits=(("dve", d2),), inc="act")
            if debug and k == 0 and sub == 0:
                add("sync", lambda e: e.dma_start(out=dbg["dbg_xtm"], in_=xln_tm),
                    waits=(("act", a4),), inc="st")
                add("sync", lambda e: e.dma_start(out=dbg["dbg_stat"], in_=stat_s),
                    waits=(("act", a4),), inc="st")
            for g in range(2):
                def tp(e, g=g):
                    tpb = ps_tp.bitcast(BF16)
                    last = None
                    for i in range(4):
                        blk = g * 4 + i
                        last = e.transpose(out=tpb[:, i * 128:(i + 1) * 128],
                                           in_=xln_tm[:, blk * 128:(blk + 1) * 128],
                                           identity=ident)
                    return last.then_inc(s_pe, 1)
                C["pe"] += 1
                pe_n = C["pe"]
                add("tensor", tp, waits=(("act", C["act"]),))
                a_ev = add("scalar", lambda e: e.activation(
                    out=cfin_b, in_=ps_tp.bitcast(BF16)[:, 0:512], func=AF.Copy),
                    waits=(("pe", pe_n), ("st", C["st"])), inc="act")
                add("sync", lambda e, g=g, sub=sub: e.dma_start(
                    out=bigscr.rearrange("p (c t) -> p c t", t=TC)[
                        :, 16 + g * 4:16 + (g + 1) * 4, sub * 128:(sub + 1) * 128],
                    in_=cfin_b.rearrange("p (c t) -> p c t", t=128)),
                    waits=(("act", a_ev),), inc="st")
        ACT_LN = C["act"]

        if k > 0:
            def halo(e):
                last = None
                for dbt in range(NDB):
                    base = dbt * (TC + 4)
                    last = e.tensor_copy(out=u_t[:, base + 1:base + 4],
                                         in_=u_t[:, base + TC + 1:base + TC + 4])
                return last.then_inc(s_dve, 1)
            C["dve"] += 1
            add("vector", halo, waits=(("act", ACT_LN),))
        DVE_HALO = C["dve"]

        for et in range(16):
            wb = et % 2
            ldn = add("sync", lambda e, et=et, wb=wb: e.dma_start(
                out=win_blk[:, wb * NDB * 128:(wb + 1) * NDB * 128].rearrange(
                    "p (kt e) -> p kt e", e=128),
                in_=win_b.rearrange("(kt p) e -> p kt e", p=128)[
                    :, :, et * 128:(et + 1) * 128]),
                waits=(("pe", max(0, C["pe"] - 1)),), inc="ld")
            for half in range(2):
                p = ps_a if half == 0 else ps_b

                def mmf(e, wb=wb, half=half, p=p):
                    last = None
                    for kt in range(NDB):
                        last = e.matmul(out=p,
                                        lhsT=win_blk[:, (wb * NDB + kt) * 128:
                                                     (wb * NDB + kt + 1) * 128],
                                        rhs=xlnf(kt)[:, half * 512:(half + 1) * 512],
                                        start=(kt == 0), stop=(kt == NDB - 1))
                    return last.then_inc(s_pe, 1)
                C["pe"] += 1
                pe_n = C["pe"]
                add("tensor", mmf, waits=(("ld", ldn), ("act", C["act"])))
                if et < 8:
                    add("scalar", lambda e, et=et, half=half, p=p: e.activation(
                        out=u_t[:, et * (TC + 4) + 4 + half * 512:
                                et * (TC + 4) + 4 + (half + 1) * 512],
                        in_=p, func=AF.Copy),
                        waits=(("pe", pe_n), ("dve", DVE_HALO)), inc="act")
                else:
                    add("scalar", lambda e, et=et, half=half, p=p: e.activation(
                        out=zs_t[:, (et - 8) * TC + half * 512:
                                (et - 8) * TC + (half + 1) * 512],
                        in_=p, func=AF.Silu),
                        waits=(("pe", pe_n),), inc="act")
        ACT_XZ = C["act"]

        for dbt in range(NDB):
            base = dbt * (TC + 4)

            def convf(e, dbt=dbt, base=base):
                cv = sp_scr.bitcast(BF16)
                tv = sg_scr.bitcast(BF16)
                e.tensor_scalar(out=cv[:, 0:TC], in0=u_t[:, base + 1:base + 1 + TC],
                                scalar1=convw_sb[:, dbt * 4:dbt * 4 + 1],
                                scalar2=convb_sb[:, dbt:dbt + 1],
                                op0=ALU.mult, op1=ALU.add)
                last = None
                for kk in range(1, 4):
                    e.tensor_scalar(out=tv[:, 0:TC],
                                    in0=u_t[:, base + 1 + kk:base + 1 + kk + TC],
                                    scalar1=convw_sb[:, dbt * 4 + kk:dbt * 4 + kk + 1],
                                    scalar2=None, op0=ALU.mult)
                    last = e.tensor_tensor(out=cv[:, 0:TC], in0=cv[:, 0:TC],
                                           in1=tv[:, 0:TC], op=ALU.add)
                return last.then_inc(s_dve, 1)
            C["dve"] += 1
            d_conv = C["dve"]
            add("vector", convf, waits=(("act", C["act"]),))
            if debug and k == 0 and dbt == 0:
                add("sync", lambda e: e.dma_start(out=dbg["dbg_cv"],
                                                  in_=sp_scr.bitcast(BF16)[:, 0:TC]),
                    waits=(("dve", d_conv),), inc="st")
            add("scalar", lambda e, dbt=dbt: e.activation(
                out=uc_t[:, dbt * TC:(dbt + 1) * TC],
                in_=sp_scr.bitcast(BF16)[:, 0:TC], func=AF.Silu),
                waits=(("dve", d_conv),), inc="act")
        ACT_UC = C["act"]

        for half in range(2):
            p = ps_a if half == 0 else ps_b

            def mmf(e, half=half, p=p):
                last = None
                for kt in range(NDB):
                    last = e.matmul(out=p[0:96, :],
                                    lhsT=wxb_sb[:, kt * 96:(kt + 1) * 96],
                                    rhs=uc_t[:, kt * TC + half * 512:
                                             kt * TC + (half + 1) * 512],
                                    start=(kt == 0), stop=(kt == NDB - 1))
                return last.then_inc(s_pe, 1)
            C["pe"] += 1
            pe_n = C["pe"]
            add("tensor", mmf, waits=(("act", ACT_UC),))
            add("vector", lambda e, half=half, p=p: e.tensor_copy(
                out=xdbl_t[0:96, half * 512:(half + 1) * 512], in_=p[0:96, :]),
                waits=(("pe", pe_n),), inc="dve")
        DVE_XP = C["dve"]

        st_before = C["st"]

        def xch(e, k=k, st_before=st_before):
            rg = get_sync_regs(e)
            with e.If(rg["role"] == 0):
                e.dma_start(out=sh_xdbl[0, k, :, :], in_=xdbl_t[0:96, :]).then_inc(s_st, 16)
            with e.Else():
                e.dma_start(out=sh_xdbl[1, k, :, :], in_=xdbl_t[0:96, :]).then_inc(s_st, 16)
            nc.end_ifs()
            e.wait_ge(s_st, st_before + 16)
            with e.If(rg["role"] == 0):
                e.dma_start(out=sh_flag[0:1, k:k + 1],
                            in_=flagv_sb[0:1, k:k + 1]).then_inc(s_st, 16)
            with e.Else():
                e.dma_start(out=sh_flag[1:2, k:k + 1],
                            in_=flagv_sb[0:1, k:k + 1]).then_inc(s_st, 16)
            nc.end_ifs()
            emit_poll(e, 1, 0, k, k, 0)
            with e.If(rg["role"] == 0):
                e.dma_start(out=xdblp_t[0:96, :], in_=sh_xdbl[1, k, :, :]).then_inc(s_ld, 16)
            with e.Else():
                e.dma_start(out=xdblp_t[0:96, :], in_=sh_xdbl[0, k, :, :]).then_inc(s_ld, 16)
            nc.end_ifs()
        C["st"] += 32
        C["ld"] += 16
        add("sync", xch, waits=(("dve", DVE_XP),))
        LD_XD = C["ld"]

        def xsum(e):
            e.tensor_tensor(out=xdbl_t[0:96, :], in0=xdbl_t[0:96, :],
                            in1=xdblp_t[0:96, :], op=ALU.add)
            e.tensor_copy(out=dtr_b, in_=xdbl_t[0:64, :])
            return e.tensor_copy(out=bc_b, in_=xdbl_t[64:96, :])
        DVE_XS = add("vector", xsum, waits=(("ld", LD_XD),), inc="dve")

        if debug and k == 0:
            add("sync", lambda e: e.dma_start(out=dbg["dbg_xln"], in_=bigscr[:, 16 * TC:24 * TC]),
                waits=(("act", ACT_LN),), inc="st")
            add("sync", lambda e: e.dma_start(out=dbg["dbg_u"], in_=u_t),
                waits=(("act", ACT_XZ),), inc="st")
            add("sync", lambda e: e.dma_start(out=dbg["dbg_zs"], in_=zs_t),
                waits=(("act", ACT_XZ),), inc="st")
            add("sync", lambda e: e.dma_start(out=dbg["dbg_uc"], in_=uc_t),
                waits=(("act", ACT_UC),), inc="st")
            add("sync", lambda e: e.dma_start(out=dbg["dbg_xdbl"][0:96, :], in_=xdbl_t[0:96, :]),
                waits=(("dve", DVE_XS),), inc="st")
            add("sync", lambda e: e.dma_start(out=dbg["dbg_m"], in_=m_sb),
                waits=(("act", ACT_M),), inc="st")

        rep_dve_hist = {}
        for s in range(32):
            dst = brep(s) if s < 16 else crep(s - 16)
            pA, pB = (ps_a, ps_b) if s % 2 == 0 else (ps_q, ps_r)

            def mmf(e, s=s, pA=pA, pB=pB):
                e.matmul(out=pA, lhsT=selrep[:, s * 128:(s + 1) * 128],
                         rhs=bc_b[0:32, 0:512], start=True, stop=True)
                return e.matmul(out=pB, lhsT=selrep[:, s * 128:(s + 1) * 128],
                                rhs=bc_b[0:32, 512:1024], start=True, stop=True
                                ).then_inc(s_pe, 1)
            C["pe"] += 1
            pe_n = C["pe"]
            wm = [("dve", DVE_XS if s < 2 else rep_dve_hist[s - 2])]
            add("tensor", mmf, waits=tuple(wm))

            def evacf(e, dst=dst, pA=pA, pB=pB):
                e.tensor_copy(out=dst[:, 0:512], in_=pA)
                return e.tensor_copy(out=dst[:, 512:1024], in_=pB)
            rep_dve_hist[s] = add("vector", evacf, waits=(("pe", pe_n),), inc="dve")
        DVE_REP = C["dve"]

        def mk_dt(dbt):
            def dtmm(e, dbt=dbt):
                e.matmul(out=ps_q, lhsT=wdtb_sb[:, dbt * 128:(dbt + 1) * 128],
                         rhs=dtr_b[:, 0:512], start=True, stop=True)
                return e.matmul(out=ps_r, lhsT=wdtb_sb[:, dbt * 128:(dbt + 1) * 128],
                                rhs=dtr_b[:, 512:1024], start=True, stop=True
                                ).then_inc(s_pe, 1)
            C["pe"] += 1
            pe_dt = C["pe"]
            add("tensor", dtmm, waits=(("dve", C["dve"]),))

            def sp1(e, dbt=dbt):
                e.tensor_scalar(out=sg_scr[:, 0:512], in0=ps_q,
                                scalar1=bdt_sb[:, dbt:dbt + 1], scalar2=None,
                                op0=ALU.add)
                e.tensor_scalar(out=sg_scr[:, 512:1024], in0=ps_r,
                                scalar1=bdt_sb[:, dbt:dbt + 1], scalar2=None,
                                op0=ALU.add)
                return e.tensor_scalar_min(out=sp_scr, in0=sg_scr, scalar1=15.0)
            d_sp1 = add("vector", sp1, waits=(("pe", pe_dt),), inc="dve")
            add("scalar", lambda e: e.activation(out=sp_scr, in_=sp_scr,
                                                 func=AF.Sigmoid, scale=-1.0),
                waits=(("dve", d_sp1),), inc="act")
            return add("scalar", lambda e: e.activation(out=sp_scr, in_=sp_scr,
                                                        func=AF.Ln), inc="act")
        a_ln_pf = mk_dt(0)

        for dbt in range(NDB):
            def sp2(e, dbt=dbt):
                e.scalar_tensor_tensor(out=dtb_t, in0=sp_scr,
                                       scalar=-1.0, in1=sg_scr,
                                       op0=ALU.mult, op1=ALU.max)
                return e.tensor_tensor(out=dtu_t, in0=dtb_t,
                                       in1=uc_t[:, dbt * TC:(dbt + 1) * TC], op=ALU.mult)
            DVE_DT = add("vector", sp2, waits=(("act", a_ln_pf),), inc="dve")
            if debug and k == 0 and dbt == 0:
                add("sync", lambda e: e.dma_start(out=dbg["dbg_dt"], in_=dtb_t),
                    waits=(("dve", DVE_DT),), inc="st")
                add("sync", lambda e: e.dma_start(out=dbg["dbg_dtu"], in_=dtu_t),
                    waits=(("dve", DVE_DT),), inc="st")
                add("sync", lambda e: e.dma_start(out=dbg["dbg_brep"], in_=brep(0)),
                    waits=(("dve", DVE_DT),), inc="st")
                add("sync", lambda e: e.dma_start(out=dbg["dbg_crep"], in_=crep(0)),
                    waits=(("dve", DVE_DT),), inc="st")

            dve_hist = {}
            pe_hist = {}
            for s in range(S):
                pb = s % 2
                sl = slice(pb * TC, (pb + 1) * TC)
                wda = [("dve", DVE_DT)]
                if s >= 2:
                    wda.append(("dve", dve_hist[s - 2]))
                a_da = add("scalar", lambda e, dbt=dbt, s=s, sl=sl: e.activation(
                    out=dA_t[:, sl], in_=dtb_t, func=AF.Exp,
                    scale=aneg_sb[:, dbt * S + s:dbt * S + s + 1]),
                    waits=tuple(wda), inc="act")

                def sloop(e, dbt=dbt, s=s, sl=sl):
                    e.tensor_tensor(out=b_t, in0=dtu_t, in1=brep(s), op=ALU.mult)
                    idx = dbt * S + s
                    e.tensor_tensor_scan(out=h_t, data0=dA_t[:, sl], data1=b_t,
                                         initial=hstate[:, idx:idx + 1],
                                         op0=ALU.mult, op1=ALU.add)
                    e.tensor_tensor(out=mm_t[:, sl], in0=h_t,
                                    in1=crep(s), op=ALU.mult)
                    return e.tensor_copy(out=hstate[:, idx:idx + 1],
                                         in_=h_t[:, TC - 1:TC])
                wsl = [("act", a_da)]
                if s >= 2:
                    wsl.append(("pe", pe_hist[s - 2]))
                d_s = add("vector", sloop, waits=tuple(wsl), inc="dve")
                dve_hist[s] = d_s
                if s == 2 and dbt + 1 < NDB:
                    a_ln_pf = mk_dt(dbt + 1)

                def ymm(e, s=s, lo=pb * TC):
                    e.matmul(out=ps_y0, lhsT=ident, rhs=mm_t[:, lo:lo + 512],
                             start=(s == 0), stop=(s == S - 1))
                    return e.matmul(out=ps_y1, lhsT=ident,
                                    rhs=mm_t[:, lo + 512:lo + 1024],
                                    start=(s == 0), stop=(s == S - 1)
                                    ).then_inc(s_pe, 1)
                C["pe"] += 1
                pe_hist[s] = C["pe"]
                add("tensor", ymm, waits=(("dve", d_s),))
            PE_YS = C["pe"]

            yb = dbt % 2

            def yfin(e, dbt=dbt, yb=yb):
                e.scalar_tensor_tensor(out=yfin_t[:, yb * TC:yb * TC + 512],
                                       in0=uc_t[:, dbt * TC:dbt * TC + 512],
                                       scalar=dvec_sb[:, dbt:dbt + 1], in1=ps_y0,
                                       op0=ALU.mult, op1=ALU.add)
                e.scalar_tensor_tensor(out=yfin_t[:, yb * TC + 512:(yb + 1) * TC],
                                       in0=uc_t[:, dbt * TC + 512:(dbt + 1) * TC],
                                       scalar=dvec_sb[:, dbt:dbt + 1], in1=ps_y1,
                                       op0=ALU.mult, op1=ALU.add)
                return e.tensor_tensor(out=yfin_t[:, yb * TC:(yb + 1) * TC],
                                       in0=yfin_t[:, yb * TC:(yb + 1) * TC],
                                       in1=zs_t[:, dbt * TC:(dbt + 1) * TC], op=ALU.mult)
            d_yf = add("vector", yfin,
                       waits=(("pe", PE_YS), ("st", C["st"] - 16 if dbt >= 2 else 0)),
                       inc="dve")

            def ystore(e, k=k, yb=yb, dbt=dbt):
                rg = get_sync_regs(e)
                with e.If(rg["role"] == 0):
                    e.dma_start(out=sh_y[0, k, dbt * 128:(dbt + 1) * 128, :],
                                in_=yfin_t[:, yb * TC:(yb + 1) * TC]).then_inc(s_st, 16)
                with e.Else():
                    e.dma_start(out=sh_y[1, k, dbt * 128:(dbt + 1) * 128, :],
                                in_=yfin_t[:, yb * TC:(yb + 1) * TC]).then_inc(s_st, 16)
                nc.end_ifs()
            C["st"] += 16
            add("sync", ystore, waits=(("dve", d_yf),))
            if debug and k == 0 and dbt == 0:
                add("sync", lambda e: e.dma_start(out=dbg["dbg_yfin"],
                                                  in_=yfin_t[:, 0:TC]),
                    waits=(("dve", d_yf),), inc="st")
                add("sync", lambda e: e.dma_start(out=dbg["dbg_hst"], in_=hstate),
                    waits=(("dve", d_yf),), inc="st")
        ST_Y = C["st"]

        def yflag(e, k=k, ST_Y=ST_Y):
            rg = get_sync_regs(e)
            e.wait_ge(s_st, ST_Y)
            with e.If(rg["role"] == 0):
                e.dma_start(out=sh_flag[2:3, k:k + 1],
                            in_=flagv_sb[0:1, NCH + k:NCH + k + 1]).then_inc(s_st, 16)
            with e.Else():
                e.dma_start(out=sh_flag[3:4, k:k + 1],
                            in_=flagv_sb[0:1, NCH + k:NCH + k + 1]).then_inc(s_st, 16)
            nc.end_ifs()
        C["st"] += 16
        add("sync", yflag)
        ST_YF = C["st"]

        ld0, st0, pe0, act0, dve0 = C["ld"], C["st"], C["pe"], C["act"], C["dve"]
        sc_parts, pc_parts, ac_parts, vc_parts = [], [], [], []

        for th in range(2):
            pe_attn_groups = []
            dve_banks_free = C["dve"]
            for kt in range(16):
                wb2 = kt % 2
                yb2 = kt % 2
                w_pe = pe_attn_groups[kt - 2] if kt >= 2 else 0
                C["ld"] += 16
                ld_w = C["ld"]

                def ldw(e, kt=kt, wb2=wb2, w_pe=w_pe):
                    if w_pe:
                        e.wait_ge(s_pe, w_pe)
                    e.dma_start(out=wout_blk[:, wb2 * DM:(wb2 + 1) * DM],
                                in_=wout_b[kt * 128:(kt + 1) * 128, :]).then_inc(s_ld, 16)
                sc_parts.append(ldw)
                C["ld"] += 16
                ld_y = C["ld"]

                def ldy(e, kt=kt, yb2=yb2, k=k, th=th):
                    e.dma_start(out=ya_t[:, yb2 * 512:(yb2 + 1) * 512],
                                in_=sh_y[kt // 8, k, (kt % 8) * 128:(kt % 8 + 1) * 128,
                                         th * 512:(th + 1) * 512]).then_inc(s_ld, 16)
                sc_parts.append(ldy)
                C["pe"] += 1
                pe_attn_groups.append(C["pe"])

                def attnmm(e, kt=kt, wb2=wb2, yb2=yb2, ld_y=ld_y,
                           dbf=dve_banks_free):
                    e.wait_ge(s_ld, ld_y)
                    if kt == 0:
                        e.wait_ge(s_dve, dbf)
                    last = None
                    for et in range(8):
                        last = e.matmul(out=banks[et],
                                        lhsT=wout_blk[:, wb2 * DM + et * 128:
                                                      wb2 * DM + (et + 1) * 128],
                                        rhs=ya_t[:, yb2 * 512:(yb2 + 1) * 512],
                                        start=(kt == 0), stop=(kt == 15))
                    last.then_inc(s_pe, 1)
                pc_parts.append(attnmm)
            PE_ATTN = C["pe"]

            for et in range(8):
                xb2 = et % 2
                C["ld"] += 16
                ld_x = C["ld"]
                dve_prev = C["dve"]

                def ldx(e, et=et, k=k, th=th, w=dve_prev if et >= 1 else 0):
                    if w:
                        e.wait_ge(s_dve, w)
                    e.dma_start(out=xfm_t,
                                in_=xbt[et * 128:(et + 1) * 128,
                                        k * TC + th * 512:k * TC + (th + 1) * 512]
                                ).then_inc(s_ld, 16)
                sc_parts.append(ldx)
                C["dve"] += 1
                d_x1 = C["dve"]

                def x1add(e, et=et, ld_x=ld_x, PE_ATTN=PE_ATTN):
                    e.wait_ge(s_ld, ld_x)
                    e.wait_ge(s_pe, PE_ATTN)
                    e.tensor_tensor(out=x1f(et), in0=xfm_t,
                                    in1=banks[et], op=ALU.add).then_inc(s_dve, 1)
                vc_parts.append(x1add)
                C["act"] += 2
                a_x1 = C["act"]

                def x1cast(e, et=et, d=d_x1):
                    e.wait_ge(s_dve, d)
                    e.activation(out=x1b(et), in_=x1f(et), func=AF.Copy).then_inc(s_act, 1)
                    e.activation(out=x1sq(et), in_=x1f(et),
                                 func=AF.Square).then_inc(s_act, 1)
                ac_parts.append(x1cast)
            ACT_X1 = C["act"]

            C["pe"] += 1
            pe_stats = C["pe"]

            def statsmm(e, ACT_X1=ACT_X1):
                e.wait_ge(s_act, ACT_X1)
                for et in range(8):
                    e.matmul(out=ps_st[0:1, :], lhsT=ones_col, rhs=x1b(et),
                             start=(et == 0), stop=(et == 7))
                last = None
                for et in range(8):
                    last = e.matmul(out=ps_q[0:1, :], lhsT=ones_col, rhs=x1sq(et),
                                    start=(et == 0), stop=(et == 7))
                last.then_inc(s_pe, 1)
            pc_parts.append(statsmm)

            C["dve"] += 1
            d_st1 = C["dve"]

            def statsv(e, pe_stats=pe_stats):
                e.wait_ge(s_pe, pe_stats)
                e.tensor_scalar_mul(out=musig[0:1, 0:512], in0=ps_st[0:1, :],
                                    scalar1=1.0 / DM)
                e.tensor_scalar_mul(out=musig[0:1, 512:1024], in0=ps_q[0:1, :],
                                    scalar1=1.0 / DM)
                e.tensor_tensor(out=musig[0:1, 1024:1536], in0=musig[0:1, 0:512],
                                in1=musig[0:1, 0:512], op=ALU.mult)
                e.tensor_tensor(out=musig[0:1, 1024:1536], in0=musig[0:1, 512:1024],
                                in1=musig[0:1, 1024:1536],
                                op=ALU.subtract).then_inc(s_dve, 1)
            vc_parts.append(statsv)
            C["act"] += 1
            a_sq = C["act"]

            def statsa(e, d=d_st1):
                e.wait_ge(s_dve, d)
                e.activation(out=musig[0:1, 512:1024], in_=musig[0:1, 1024:1536],
                             func=AF.Sqrt, bias=eps_t[0:1, :]).then_inc(s_act, 1)
            ac_parts.append(statsa)
            C["dve"] += 1
            d_st2 = C["dve"]

            def statsv2(e, a=a_sq):
                e.wait_ge(s_act, a)
                e.reciprocal(out=musig[0:1, 1024:1536], in_=musig[0:1, 512:1024])
                e.tensor_scalar_mul(out=musig[0:1, 512:1024], in0=musig[0:1, 1024:1536],
                                    scalar1=-1.0)
                e.tensor_copy(out=rowbf[0:1, 0:512], in_=musig[0:1, 512:1024])
                e.tensor_copy(out=rowbf[0:1, 512:1024],
                              in_=musig[0:1, 0:512]).then_inc(s_dve, 1)
            vc_parts.append(statsv2)
            C["pe"] += 1
            pe_rep = C["pe"]

            def repmm(e, d=d_st2):
                e.wait_ge(s_dve, d)
                e.matmul(out=ps_a, lhsT=ones_row, rhs=rowbf[0:1, 0:512],
                         start=True, stop=True)
                e.matmul(out=ps_b, lhsT=ones_row, rhs=rowbf[0:1, 512:1024],
                         start=True, stop=True).then_inc(s_pe, 1)
            pc_parts.append(repmm)
            C["dve"] += 1
            d_rep = C["dve"]

            def repev(e, p=pe_rep):
                e.wait_ge(s_pe, p)
                e.tensor_copy(out=s2rep, in_=ps_a)
                e.tensor_copy(out=mu2rep, in_=ps_b).then_inc(s_dve, 1)
            vc_parts.append(repev)

            d_fin_prev = {}
            for ot in range(8):
                pbank = banks[ot % 2]
                w_fin = d_fin_prev.get(ot - 2, d_rep)
                C["pe"] += 1
                pe_ffn = C["pe"]

                def ffnmm(e, ot=ot, pbank=pbank, w=w_fin):
                    e.wait_ge(s_dve, w)
                    last = None
                    for kt in range(8):
                        last = e.matmul(out=pbank,
                                        lhsT=m_sb[:, kt * DM + ot * 128:
                                                  kt * DM + (ot + 1) * 128],
                                        rhs=x1b(kt), start=(kt == 0), stop=(kt == 7))
                    last.then_inc(s_pe, 1)
                pc_parts.append(ffnmm)
                C["dve"] += 1
                d_fin = C["dve"]
                d_fin_prev[ot] = d_fin
                st_w = C["st"] if (ot > 0 or th > 0) else 0

                def ffnfin(e, ot=ot, pbank=pbank, p=pe_ffn, st_w=st_w):
                    e.wait_ge(s_pe, p)
                    if st_w:
                        e.wait_ge(s_st, st_w)
                    e.scalar_tensor_tensor(out=cfin_b, in0=mu2rep,
                                           scalar=mbar_sb[:, ot:ot + 1], in1=pbank,
                                           op0=ALU.mult, op1=ALU.subtract)
                    e.tensor_tensor(out=cfin_b, in0=cfin_b, in1=s2rep, op=ALU.mult)
                    e.scalar_tensor_tensor(out=outet_t,
                                           in0=cfin_b, scalar=fbias_sb[:, ot:ot + 1],
                                           in1=x1f(ot), op0=ALU.add,
                                           op1=ALU.add).then_inc(s_dve, 1)
                vc_parts.append(ffnfin)
                C["st"] += 16

                def ostore(e, ot=ot, k=k, th=th, d=d_fin):
                    e.wait_ge(s_dve, d)
                    e.dma_start(out=out_d[k // 2, ot * 128:(ot + 1) * 128,
                                          th * 512:(th + 1) * 512],
                                in_=outet_t).then_inc(s_st, 16)
                sc_parts.append(ostore)

        n_ld_c = C["ld"] - ld0
        n_st_c = C["st"] - st0
        n_pe_c = C["pe"] - pe0
        n_act_c = C["act"] - act0
        n_dve_c = C["dve"] - dve0

        def sync_c(e, k=k, par=par, parts=tuple(sc_parts), n_ld=n_ld_c, n_st=n_st_c,
                   ST_YF=ST_YF):
            rg = get_sync_regs(e)
            cond = (rg["role"] == 0) if par == 0 else (rg["role"] == 1)
            with e.If(cond):
                row = 3 if par == 0 else 2
                e.reg_load(rg["tgt"], flagv_sb[0:1, NCH + k:NCH + k + 1])
                tgtv = make_scalar_value(rg["tgt"], min_val=0, max_val=(1 << 30))
                e.reg_mov(rg["fl"], 0)
                flv = make_scalar_value(rg["fl"], min_val=0, max_val=(1 << 30))
                cntv = make_scalar_value(rg["cnt"], min_val=0, max_val=(1 << 30))
                with e.While(lambda: flv != tgtv):
                    e.dma_start(out=pollbuf[0:1, 1:2],
                                in_=sh_flag[row:row + 1, k:k + 1]).then_inc(s_pol, 16)
                    e.reg_add(rg["cnt"], rg["cnt"], 16)
                    e.wait_ge(s_pol, cntv)
                    e.reg_load(rg["fl"], pollbuf[0:1, 1:2])
                e.wait_ge(s_st, ST_YF)
                for p in parts:
                    p(e)
            with e.Else():
                e.sem_inc(s_ld, n_ld)
                e.sem_inc(s_st, n_st)
            nc.end_ifs()
        add("sync", sync_c)

        def mk_wrap(parts, sem_, n, par=par, k=k):
            parts = tuple(parts)
            name = f"rw{k}_{id(parts) % 100000}"

            def f(e, name=name):
                r = e.alloc_register(name)
                e.wait_ge(s_ld, LD_PRO)
                e.reg_load(r, role_sb[0:1, 0:1])
                rv = make_scalar_value(r, min_val=0, max_val=1)
                cond = (rv == 0) if par == 0 else (rv == 1)
                with e.If(cond):
                    for p in parts:
                        p(e)
                with e.Else():
                    if n > 0:
                        e.sem_inc(sem_, n)
                nc.end_ifs()
                e.free_register(r)
            return f

        add("tensor", mk_wrap(pc_parts, s_pe, n_pe_c))
        add("scalar", mk_wrap(ac_parts, s_act, n_act_c))
        add("vector", mk_wrap(vc_parts, s_dve, n_dve_c))

    FINAL_ST = C["st"]
    add("sync", lambda e: e.wait_ge(s_st, FINAL_ST))

    with nc.Block() as block:
        @block.sync
        def _(e):
            for f in steps["sync"]:
                f(e)

        @block.tensor
        def _(e):
            for f in steps["tensor"]:
                f(e)

        @block.scalar
        def _(e):
            for f in steps["scalar"]:
                f(e)

        @block.vector
        def _(e):
            for f in steps["vector"]:
                f(e)

    for g in reversed(ctx):
        g.__exit__(None, None, None)
    return nc



def _selrep():
    m = np.zeros((32, 32 * 128), dtype=np.float32)
    for s in range(32):
        m[s, s * 128:(s + 1) * 128] = 1.0
    return m.astype(bf16)


def _dft_tables():
    n = np.arange(DM, dtype=np.float64)
    ang = 2.0 * np.pi / DM * np.outer(n, n)
    ct = np.cos(ang).astype(np.float32).astype(bf16)
    st = np.sin(ang).astype(np.float32).astype(bf16)
    return ct, st


def _per_core_inputs(inputs, c, T, nonce):
    b, j = c // 2, c % 2
    NCH = T // TC
    f32 = np.float32
    W_in = np.asarray(inputs["L0_attn_W_in"], f32)
    W_x = np.asarray(inputs["L0_attn_W_x"], f32)
    W_dt = np.asarray(inputs["L0_attn_W_dt"], f32)
    W_out = np.asarray(inputs["L0_attn_W_out"], f32)
    x = np.ascontiguousarray(np.asarray(inputs["L0_attn_x"], f32)[b, :T])
    lo, hi = j * DH, (j + 1) * DH

    def tile128(v):
        return np.ascontiguousarray(v.reshape(NDB, 128).T)

    ct, st = _dft_tables()
    sel = np.concatenate([W_in[lo:hi], W_in[DI + lo:DI + hi]], axis=0)
    alog = np.asarray(inputs["L0_attn_A_log"], f32)[lo:hi]
    alog_t = np.ascontiguousarray(
        alog.reshape(NDB, 128, S).transpose(1, 0, 2).reshape(128, NDB * S))
    convw = np.asarray(inputs["L0_attn_conv_w"], f32)[lo:hi]
    convw_t = np.ascontiguousarray(
        convw.reshape(NDB, 128, 4).transpose(1, 0, 2).reshape(128, NDB * 4))
    flagv = (np.arange(1, 2 * NCH + 1, dtype=np.uint64) + np.uint64(nonce)) % np.uint64(2**32)
    return {
        "xb": x,
        "xbt": np.ascontiguousarray(x.T),
        "win_t": np.ascontiguousarray(sel.T),
        "wx_t": np.ascontiguousarray(W_x[:, lo:hi].T),
        "wdt_t": np.ascontiguousarray(W_dt[lo:hi].T),
        "bdt_in": tile128(np.asarray(inputs["L0_attn_b_dt"], f32)[lo:hi]),
        "alog_in": alog_t,
        "convw_in": convw_t,
        "convb_in": tile128(np.asarray(inputs["L0_attn_conv_b"], f32)[lo:hi]),
        "dvec_in": tile128(np.asarray(inputs["L0_attn_D"], f32)[lo:hi]),
        "wout_t": np.ascontiguousarray(W_out.T),
        "wr_t": np.ascontiguousarray(np.asarray(inputs["L0_ffn_w_freq_real"], f32).T),
        "wi_t": np.ascontiguousarray(np.asarray(inputs["L0_ffn_w_freq_imag"], f32).T),
        "ct_in": ct,
        "st_in": st,
        "fbias_in": tile128(np.asarray(inputs["L0_ffn_bias"], f32)),
        "ident_in": np.eye(128, dtype=np.float32).astype(bf16),
        "selrep_in": _selrep(),
        "role_in": np.array([[j, 0, 0, 0]], dtype=np.uint32),
        "flagv_in": flagv.astype(np.uint32).reshape(1, -1),
    }


_CACHE = {}


def kernel(**inputs):
    T = int(np.asarray(inputs["L0_attn_x"]).shape[1])
    if T not in _CACHE:
        _CACHE[T] = build_program(T)
    nc = _CACHE[T]
    nonce = int.from_bytes(np.random.bytes(4), "little")
    in_maps = [_per_core_inputs(inputs, c, T, nonce) for c in range(8)]
    res = run_bass_kernel_spmd(nc, in_maps, core_ids=list(range(8)))
    NCH = T // TC
    out = np.empty((B, T, DM), dtype=np.float32)
    for c in range(8):
        b, j = c // 2, c % 2
        o = res.results[c]["out_d"]
        for i in range(NCH // 2):
            k = 2 * i + j
            out[b, k * TC:(k + 1) * TC, :] = o[i].T
    return out
